# version 09
# baseline (speedup 1.0000x reference)
import numpy as np
import ml_dtypes
from contextlib import ExitStack
import concourse.bass as bass
import concourse.mybir as mybir
from concourse.bass_utils import run_bass_kernel_spmd

F32 = mybir.dt.float32
BF16 = mybir.dt.bfloat16
I32 = mybir.dt.int32
U16 = mybir.dt.uint16
A = mybir.AluOpType
AF = mybir.ActivationFunctionType
AX = mybir.AxisListType

R = 8
B_, S, D = 4, 2048, 2048
T = B_ * S
TKC = T // R
FT = 256
ISQ = 0.08838834764831845


def build_nc():
    nc = bass.Bass("TRN2", target_bir_lowering=False)

    def din(name, shape, dt):
        return nc.dram_tensor(name, shape, dt, kind="ExternalInput").ap()

    def dint(name, shape, dt):
        return nc.dram_tensor(name, shape, dt, kind="Internal").ap()

    x_tok = din("x_tok", [TKC, D], F32)
    xT_s = din("xT_s", [FT, T], F32)
    w_qkv_c = [din(f"w{p}_c", [D, FT], F32) for p in "qkv"]
    w_qkv_se = [din(f"w{p}_se", [D, FT], F32) for p in "qkv"]
    wo_c = din("wo_c", [FT, D], F32)
    wo_se = din("wo_se", [FT, D], F32)
    bqkv_h = din("bqkv", [128, 6], F32)
    bo_h = din("bo_s", [128, 2], F32)
    ffn_p = {w: din(f"{w}_p", [2048, 1024], I32) for w in ("g", "u", "d")}
    ffn_s = {w: din(f"{w}_s", [2048, 64], F32) for w in ("g", "u", "d")}
    ident_h = din("ident", [128, 128], BF16)
    yT_s = nc.dram_tensor("yT_s", [FT, T], F32, kind="ExternalOutput").ap()

    ag_in = dint("ag_in", [D, TKC], BF16)
    ag_out = [nc.dram_tensor(f"ag_out{h}", [R * 1024, TKC], BF16,
                             kind="Internal").ap() for h in range(2)]
    dec = {w: dint(f"dec_{w}", [16, 128, 2048], BF16) for w in ("g", "u", "d")}
    dec_a = dint("dec_a", [8, 128, 2048], BF16)
    part = [dint(f"part{c}", [D, S], F32) for c in range(4)]
    rs_o = [dint(f"rs_o{c}", [FT, S], F32) for c in range(4)]

    with ExitStack() as ctx:
        def sbuf(name, shape, dt):
            return ctx.enter_context(nc.sbuf_tensor(name, shape, dt))

        def psum(name, shape, dt):
            return ctx.enter_context(nc.psum_tensor(name, shape, dt))

        xnt = sbuf("xnt", [128, 32768], BF16)
        uni = sbuf("uni", [128, 32768], BF16)
        ao = sbuf("ao", [128, 4096], BF16)
        wpool = sbuf("wpool", [128, 8192], BF16)
        apb = sbuf("apb", [128, 4096], BF16)
        pk = sbuf("pk", [128, 2048], I32)
        dtmp = sbuf("dtmp", [128, 4096], I32)
        codes = sbuf("codes", [128, 2048], U16)
        dq = sbuf("dq", [128, 2048], BF16)
        stage = sbuf("stage", [128, 2048], BF16)
        scb = sbuf("scb", [128, 128], F32)
        id_sb = sbuf("id_sb", [128, 128], BF16)
        bqkv_sb = sbuf("bqkv_sb", [128, 6], F32)
        bo_sb = sbuf("bo_sb", [128, 2], F32)
        gg = sbuf("gg", [128, 1024], BF16)
        yp = sbuf("yp", [128, 1024], F32)
        smal = sbuf("smal", [128, 16], F32)
        xn_t = stage

        P1 = psum("P1", [128, 2048], F32)
        P2 = [psum("P2a", [128, 512], F32), psum("P2b", [128, 512], F32)]
        P3 = [psum("P3a", [128, 1024], BF16), psum("P3b", [128, 1024], BF16)]

        semnames = ["ld_sp", "ld_act", "ld_pool", "cc", "pe", "dve", "act", "pool"]
        sem = {n: ctx.enter_context(nc.semaphore(n)) for n in semnames}
        cnt = {n: 0 for n in semnames}
        ops = {"sync": [], "vector": [], "scalar": [], "tensor": [], "gpsimd": []}

        def em(eng, fn):
            ops[eng].append(fn)

        def W(eng, s, v):
            if v > 0:
                em(eng, lambda e, _s=sem[s], _v=v: e.wait_ge(_s, _v))

        def inc(s, n=1):
            cnt[s] += n
            return cnt[s]

        def DMA(q, sname, out_ap, in_ap, waits=()):
            for (s, v) in waits:
                W(q, s, v)
            c = inc(sname, 16)
            em(q, lambda e, o=out_ap, i=in_ap, sm=sem[sname]:
               e.dma_start(out=o, in_=i).then_inc(sm, 16))
            return c

        def OP(eng, build, incs=None, waits=()):
            for (s, v) in waits:
                W(eng, s, v)
            if incs:
                c = inc(incs)
                em(eng, lambda e, b=build, sm=sem[incs]: b(e).then_inc(sm, 1))
                return c
            em(eng, lambda e, b=build: b(e))
            return None

        u_f32 = uni[:].bitcast(F32)
        wp_f32 = wpool[:].bitcast(F32)
        jnk = codes[:].bitcast(BF16)
        fnb = dtmp[:].bitcast(F32)

        U_QT, U_KT, U_V, U_P, U_PN, U_VT, U_PT = 0, 4096, 8192, 12288, 14336, 16384, 16384

        p1f = [[("act", 0)], [("act", 0)], [("act", 0)], [("act", 0)]]
        p2f = [("act", 0), ("act", 0)]
        p3f = [("act", 0), ("act", 0)]

        for o, i in ((id_sb, ident_h), (bqkv_sb, bqkv_h), (bo_sb, bo_h)):
            con_ld = DMA("sync", "ld_sp", o[:], i)

        OP("vector", lambda v: v.memset(smal[:, 10:11], 1e-5))
        OP("vector", lambda v: v.memset(smal[:, 11:12], 16128.0), incs="dve")
        ln_done = []
        pe_tr = 0
        for tt in range(8):
            slot = tt % 2
            xsl = wp_f32[:, slot * 2048:(slot + 1) * 2048]
            ldw = [("dve", ln_done[tt - 2])] if tt >= 2 else ()
            ldv = DMA("sync", "ld_sp", xsl, x_tok[tt * 128:(tt + 1) * 128, :], waits=ldw)
            rs1 = OP("vector", lambda v, x=xsl: v.reduce_sum(smal[:, 0:1], x, axis=AX.X),
                     incs="dve", waits=[("ld_sp", ldv)])
            nmn = OP("vector", lambda v: v.tensor_single_scalar(smal[:, 2:3], smal[:, 0:1],
                     -1.0 / D, A.mult), incs="dve", waits=[("dve", rs1)])
            sq = OP("scalar", lambda s, x=xsl: s.activation(out=jnk[:], in_=x, func=AF.Square,
                    accum_out=smal[:, 1:2]), incs="act", waits=[("ld_sp", ldv)])
            OP("vector", lambda v: v.tensor_single_scalar(smal[:, 3:4], smal[:, 1:2],
                                                          1.0 / D, A.mult),
               waits=[("act", sq), ("dve", nmn)])
            m2c = OP("vector", lambda v: v.tensor_tensor(smal[:, 4:5], smal[:, 2:3],
                     smal[:, 2:3], A.mult), incs="dve")
            vd = OP("vector", lambda v: v.tensor_tensor(smal[:, 12:13], smal[:, 3:4], smal[:, 4:5],
                    A.subtract), incs="dve", waits=[("dve", m2c)])
            sdc = OP("scalar", lambda s: s.activation(out=smal[:, 13:14], in_=smal[:, 12:13],
                     func=AF.Sqrt, bias=smal[:, 10:11], scale=1.0), incs="act",
                     waits=[("dve", vd)])
            rcc = OP("vector", lambda v: v.reciprocal(smal[:, 5:6], smal[:, 13:14]),
                     incs="dve", waits=[("act", sdc)])
            xw = [("dve", rcc)] + ([("pe", pe_tr)] if tt >= 1 else [])
            xnd = OP("vector", lambda v, x=xsl: v.tensor_scalar(xn_t[:], x, smal[:, 2:3],
                     smal[:, 5:6], A.add, A.mult), incs="dve", waits=xw)
            ln_done.append(xnd)
            W("tensor", "dve", xnd)
            if tt == 0:
                W("tensor", "ld_sp", con_ld)
            for half in range(2):
                pd = P3[half]
                W("tensor", p3f[half][0], p3f[half][1])
                for k in range(8):
                    dtl = half * 8 + k
                    pe_tr = OP("tensor", lambda t, p=pd[:, k * 128:(k + 1) * 128],
                               i=xn_t[:, dtl * 128:(dtl + 1) * 128]:
                               t.transpose(p, i, id_sb[:]),
                               incs="pe" if k == 7 else None) or pe_tr
                dst = uni[:].rearrange("p (dt q) -> p dt q", q=1024)[
                    :, half * 8:half * 8 + 8, tt * 128:(tt + 1) * 128]
                ev = OP("scalar", lambda s, o=dst,
                        i=pd[:].rearrange("p (dt q) -> p dt q", q=128):
                        s.copy(out=o, in_=i), incs="act", waits=[("pe", pe_tr)])
                p3f[half] = ("act", ev)
        W("gpsimd", "act", cnt["act"])
        for dtl in range(16):
            ag_st = DMA("gpsimd", "ld_pool", ag_in[dtl * 128:(dtl + 1) * 128, :],
                        uni[:, dtl * 1024:(dtl + 1) * 1024])
        W("gpsimd", "ld_pool", ag_st)
        ag_done = [0, 0]
        for h in range(2):
            ag_done[h] = OP("gpsimd", lambda g, _h=h: g.collective_compute(
                "AllGather", A.bypass, replica_groups=[list(range(R))],
                ins=[ag_in[1024 * _h:1024 * (_h + 1), :].opt()],
                outs=[ag_out[_h].opt()]), incs="cc")

        a1_prev_tt = 0
        st_free2 = [("ld_act", 0), ("ld_act", 0)]
        a1_srcs = [(w_qkv_c[0], w_qkv_se[0]), (w_qkv_c[1], w_qkv_se[1]),
                   (w_qkv_c[2], w_qkv_se[2]), (wo_c, wo_se)]
        sidx = 0
        for pi in range(4):
            ch, sh = a1_srcs[pi]
            if pi < 3:
                cin = ch[:].rearrange("(dt p) f -> p dt f", p=128)
                sin = sh[:].rearrange("(dt p) f -> p dt f", p=128)
                DMA("sync", "ld_sp",
                    u_f32[:, 0:4096].rearrange("p (dt f) -> p dt f", f=FT), cin,
                    waits=[("ld_pool", ag_st), ("dve", a1_prev_tt)])
                sl = DMA("sync", "ld_sp",
                         u_f32[:, 4096:8192].rearrange("p (dt f) -> p dt f", f=FT), sin)
                nft, wid = 2, 2048
            else:
                DMA("sync", "ld_sp", u_f32[:, 0:2048], ch[0:128, :],
                    waits=[("dve", a1_prev_tt)])
                DMA("sync", "ld_sp", u_f32[:, 2048:4096], ch[128:256, :])
                DMA("sync", "ld_sp", u_f32[:, 4096:6144], sh[0:128, :])
                sl = DMA("sync", "ld_sp", u_f32[:, 6144:8192], sh[128:256, :])
                nft, wid = 2, 2048
            W("vector", "ld_sp", sl)
            for f in range(nft):
                for hg in range(2):
                    ss = sidx % 2
                    sidx += 1
                    W("vector", st_free2[ss][0], st_free2[ss][1])
                    a1_tt = 0
                    for k in range(8):
                        dtl = hg * 8 + k
                        if pi < 3:
                            a_ = u_f32[:, dtl * 256 + f * 128:dtl * 256 + f * 128 + 128]
                            b_ = u_f32[:, 4096 + dtl * 256 + f * 128:4096 + dtl * 256 + f * 128 + 128]
                        else:
                            a_ = u_f32[:, f * 2048 + dtl * 128:f * 2048 + (dtl + 1) * 128]
                            b_ = u_f32[:, 4096 + f * 2048 + dtl * 128:4096 + f * 2048 + (dtl + 1) * 128]
                        o = stage[:, ss * 1024 + k * 128:ss * 1024 + (k + 1) * 128]
                        a1_tt = OP("vector", lambda v, _o=o, _a=a_, _b=b_:
                                   v.tensor_tensor(_o, _a, _b, A.mult),
                                   incs="dve" if k == 7 else None) or a1_tt
                    ft_ix = 2 * pi + f
                    stv = DMA("scalar", "ld_act", dec_a[ft_ix][:, hg * 1024:(hg + 1) * 1024],
                              stage[:, ss * 1024:(ss + 1) * 1024], waits=[("dve", a1_tt)])
                    st_free2[ss] = ("ld_act", stv)
            a1_prev_tt = cnt["dve"]
        a1_all = cnt["dve"]
        a1_prev_st = cnt["ld_act"]

        Tt = [dtmp[:, i * 1024:(i + 1) * 1024] for i in range(4)]
        pk_free = [("dve", 0), ("dve", 0)]
        sc_free = [("dve", 0), ("dve", 0)]
        st_free = st_free2
        dec_store = {}
        dec_i = [0]
        dec_order = [(w, rt) for rt in range(16) for w in ("g", "u")] + \
                    [("d", rt) for rt in range(16)]
        p0_act = cnt["act"]

        def emit_decode(n):
            for _ in range(n):
                if dec_i[0] >= len(dec_order):
                    return
                i = dec_i[0]
                dec_i[0] += 1
                w, rt = dec_order[i]
                slot = i % 2
                pks = pk[:, slot * 1024:(slot + 1) * 1024]
                scs = scb[:, slot * 64:(slot + 1) * 64]
                pkv = DMA("gpsimd", "ld_pool", pks, ffn_p[w][rt * 128:(rt + 1) * 128, :],
                          waits=[pk_free[slot]])
                scv = DMA("gpsimd", "ld_pool", scs, ffn_s[w][rt * 128:(rt + 1) * 128, :],
                          waits=[sc_free[slot]])
                w0 = [("ld_pool", pkv)] + ([("act", p0_act)] if i == 0 else [])
                for lohi in range(2):
                    if lohi == 0:
                        OP("vector", lambda v, p=pks: v.tensor_single_scalar(
                           Tt[0][:], p, 7, A.bitwise_and), waits=w0)
                        xtd = OP("vector", lambda v, p=pks: v.tensor_single_scalar(
                                 Tt[1][:], p, 8, A.bitwise_and), incs="dve")
                    else:
                        OP("vector", lambda v, p=pks: v.tensor_scalar(
                           Tt[0][:], p, 4, 7, A.logical_shift_right, A.bitwise_and))
                        pkfree = OP("vector", lambda v, p=pks: v.tensor_scalar(
                                    Tt[1][:], p, 4, 8, A.logical_shift_right, A.bitwise_and),
                                    incs="dve")
                        pk_free[slot] = ("dve", pkfree)
                        xtd = pkfree
                    bb = OP("scalar", lambda s: s.activation(out=Tt[2][:], in_=Tt[0][:],
                            func=AF.Identity, scale=64.0, bias=smal[:, 11:12]),
                            incs="act", waits=[("dve", xtd)])
                    mmn = OP("vector", lambda v: v.scalar_tensor_tensor(Tt[3][:], Tt[0][:],
                             16128.0, Tt[2][:], A.mult, A.min), incs="dve",
                             waits=[("act", bb)])
                    dst = codes[:, lohi:2048:2]
                    dcd = OP("vector", lambda v, _d=dst: v.scalar_tensor_tensor(
                             _d, Tt[1][:], 4096.0, Tt[3][:], A.mult, A.add), incs="dve",
                             waits=[("dve", mmn)])
                dqd = OP("vector", lambda v, _s=scs:
                         v.tensor_tensor(dq[:].rearrange("p (j r) -> p j r", r=32),
                                         codes[:].bitcast(BF16).rearrange("p (j r) -> p j r", r=32),
                                         _s.unsqueeze(2).broadcast_to([128, 64, 32]),
                                         A.mult), incs="dve",
                         waits=[("ld_pool", scv), ("dve", dcd)])
                sc_free[slot] = ("dve", dqd)
                W("tensor", "dve", dqd)
                for half in range(2):
                    pd = P3[half]
                    W("tensor", p3f[half][0], p3f[half][1])
                    ptr = 0
                    for k in range(8):
                        kt2 = half * 8 + k
                        ptr = OP("tensor", lambda t, p=pd[:, k * 128:(k + 1) * 128],
                                 i2=dq[:, kt2 * 128:(kt2 + 1) * 128]:
                                 t.transpose(p, i2, id_sb[:]),
                                 incs="pe" if k == 7 else None) or ptr
                    ss2 = (2 * i + half) % 2
                    ev = OP("scalar", lambda s, o=stage[:, ss2 * 1024:(ss2 + 1) * 1024],
                            i2=pd[:]: s.copy(out=o, in_=i2), incs="act",
                            waits=[("pe", ptr), st_free[ss2]])
                    p3f[half] = ("act", ev)
                    stv = DMA("scalar", "ld_act", dec[w][rt][:, half * 1024:(half + 1) * 1024],
                              stage[:, ss2 * 1024:(ss2 + 1) * 1024], waits=[("act", ev)])
                    st_free[ss2] = ("ld_act", stv)
                dec_store[(w, rt)] = stv

        xnt_ld = [0, 0, 0, 0]
        xnt_ld_h0 = [0, 0, 0, 0]
        apb_use = [("pe", 0), ("pe", 0)]
        wslot_use = [("pe", 0), ("pe", 0), ("pe", 0), ("pe", 0)]
        yp_free = [("ld_act", 0), ("ld_act", 0)]
        b5_last = [0, 0, 0, 0]
        rs_done = [0, 0, 0, 0]
        fn_free = [("dve", 0), ("dve", 0)]
        gg_free = [("dve", 0), ("dve", 0)]
        qk_ev = [0]
        ao_ev = [0]
        b5_intd = [0]

        def emit_xnt_load(c):
            w0 = [("pe", b5_last[c - 1])] if c > 0 else []
            half_ld = [0, 0]
            for half in range(2):
                waits = w0 + ([("cc", ag_done[1])] if c == 0 else [])
                w0 = []
                for dtl in range(8 * half, 8 * half + 8):
                    for h2 in range(2):
                        half_ld[half] = DMA(
                            "sync", "ld_sp",
                            xnt[:, dtl * 2048 + h2 * 1024:dtl * 2048 + (h2 + 1) * 1024],
                            ag_out[half][(2 * c + h2) * 1024 + (dtl - 8 * half) * 128:
                                         (2 * c + h2) * 1024 + (dtl - 8 * half + 1) * 128, :],
                            waits=waits)
                        waits = ()
            xnt_ld[c] = half_ld[1]
            xnt_ld_h0[c] = half_ld[0]

        def emit_B2(c):
            last_ev = 0
            for ft in range(6):
                slot = ft % 2
                apb_ld = DMA("sync", "ld_sp", apb[:, slot * 2048:(slot + 1) * 2048],
                             dec_a[ft], waits=[("ld_act", a1_prev_st), apb_use[slot]])
                for tg in range(4):
                    g = (4 * ft + tg) % 2
                    pp = P2[g]
                    W("tensor", "ld_sp", max(apb_ld, xnt_ld_h0[c]))
                    W("tensor", p2f[g][0], p2f[g][1])
                    cp = 0
                    for dtl in range(16):
                        if dtl == 8:
                            W("tensor", "ld_sp", xnt_ld[c])
                        cp = OP("tensor", lambda t, _p=pp[:], _w=apb[:, slot * 2048 + dtl * 128:
                                slot * 2048 + (dtl + 1) * 128],
                                _x=xnt[:, dtl * 2048 + tg * 512:dtl * 2048 + (tg + 1) * 512],
                                st=(dtl == 0), sp=(dtl == 15): t.matmul(_p, _w, _x, start=st, stop=sp),
                                incs="pe" if dtl == 15 else None) or cp
                    if tg == 3:
                        apb_use[slot] = ("pe", cp)
                    if ft < 2:
                        dst = uni[:, U_QT + ft * 2048 + tg * 512:U_QT + ft * 2048 + (tg + 1) * 512]
                    elif ft < 4:
                        dst = uni[:, U_KT + (ft - 2) * 2048 + tg * 512:U_KT + (ft - 2) * 2048 + (tg + 1) * 512]
                    else:
                        dst = uni[:, U_VT + (ft - 4) * 2048 + tg * 512:U_VT + (ft - 4) * 2048 + (tg + 1) * 512]
                    ev = OP("scalar", lambda s, o=dst, i2=pp[:], bi=bqkv_sb[:, ft:ft + 1]:
                            s.activation(out=o, in_=i2, func=AF.Identity, bias=bi),
                            incs="act", waits=[("pe", cp)])
                    p2f[g] = ("act", ev)
                    last_ev = ev
            qk_ev[0] = last_ev

        def emit_B3(c):
            W("tensor", "act", qk_ev[0])
            for h in range(2):
                for half in range(2):
                    pd = P3[half]
                    W("tensor", p3f[half][0], p3f[half][1])
                    ptr = 0
                    for k in range(8):
                        tk = half * 8 + k
                        ptr = OP("tensor", lambda t, p=pd[:, k * 128:(k + 1) * 128],
                                 i2=uni[:, U_VT + h * 2048 + tk * 128:U_VT + h * 2048 + (tk + 1) * 128]:
                                 t.transpose(p, i2, id_sb[:]),
                                 incs="pe" if k == 7 else None) or ptr
                    dstv = uni[:, U_V + half * 2048:U_V + (half + 1) * 2048].rearrange(
                        "p (a r) -> p a r", r=256)[:, :, h * 128:(h + 1) * 128]
                    ev = OP("scalar", lambda s, o=dstv,
                            i2=pd[:].rearrange("p (a r) -> p a r", r=128):
                            s.copy(out=o, in_=i2), incs="act", waits=[("pe", ptr)])
                    p3f[half] = ("act", ev)

        def emit_B4(c, h, avd):
            prev = {"exp": 0, "nm": 0}
            ptev = {}
            for q in range(16):
                if q == 0:
                    W("tensor", "act", qk_ev[0])
                    for k in range(4):
                        for bb in p1f[k]:
                            W("tensor", bb[0], bb[1])
                else:
                    W("tensor", "act", prev["exp"])
                    W("tensor", "dve", prev["nm"])
                sc0 = sc3 = 0
                for kt in range(4):
                    r_ = OP("tensor", lambda t, _p=P1[:, kt * 512:(kt + 1) * 512],
                            _q=uni[:, U_QT + h * 2048 + q * 128:U_QT + h * 2048 + (q + 1) * 128],
                            _k=uni[:, U_KT + h * 2048 + kt * 512:U_KT + h * 2048 + (kt + 1) * 512]:
                            t.matmul(_p, _q, _k, start=True, stop=True),
                            incs="pe" if kt in (0, 3) else None)
                    if kt == 0:
                        sc0 = r_
                    elif kt == 3:
                        sc3 = r_
                rmx = OP("vector", lambda v: v.reduce_max(smal[:, 6:7], P1[:, 0:512], axis=AX.X,
                         negate=True), incs="dve", waits=[("pe", sc0), ("act", prev["exp"])])
                nm = OP("vector", lambda v: v.tensor_single_scalar(smal[:, 7:8], smal[:, 6:7],
                        ISQ, A.mult), incs="dve", waits=[("dve", rmx)])
                expd = OP("scalar", lambda s: s.activation(out=uni[:, U_P:U_P + 2048],
                          in_=P1[:], func=AF.Exp, scale=ISQ, bias=smal[:, 7:8],
                          accum_out=smal[:, 8:9]), incs="act",
                          waits=[("pe", sc3), ("dve", nm)])
                rcp = OP("vector", lambda v: v.reciprocal(smal[:, 9:10], smal[:, 8:9]),
                         incs="dve", waits=[("act", expd)])
                pn = OP("vector", lambda v: v.tensor_single_scalar(uni[:, U_PN:U_PN + 2048],
                        uni[:, U_P:U_P + 2048], smal[:, 9:10], A.mult), incs="dve",
                        waits=[("dve", rcp)])
                prev = {"exp": expd, "nm": nm}
                ql = q % 4
                W("tensor", "dve", pn)
                for half in range(2):
                    pd = P3[half]
                    W("tensor", p3f[half][0], p3f[half][1])
                    ptr = 0
                    for k in range(8):
                        kt = half * 8 + k
                        ptr = OP("tensor", lambda t, p=pd[:, k * 128:(k + 1) * 128],
                                 i2=uni[:, U_PN + kt * 128:U_PN + (kt + 1) * 128]:
                                 t.transpose(p, i2, id_sb[:]),
                                 incs="pe" if k == 7 else None) or ptr
                    dstv = uni[:, U_PT + half * 8 * 512:U_PT + (half * 8 + 8) * 512].rearrange(
                        "p (a r) -> p a r", r=512)[:, :, ql * 128:(ql + 1) * 128]
                    ev = OP("scalar", lambda s, o=dstv,
                            i2=pd[:].rearrange("p (a r) -> p a r", r=128):
                            s.copy(out=o, in_=i2), incs="act",
                            waits=[("pe", ptr)] + ([("pe", avd)] if half == 0 else []))
                    p3f[half] = ("act", ev)
                ptev[q] = ("act", cnt["act"])
                if ql == 3:
                    g4 = q // 4
                    gs = g4 % 2
                    W("tensor", ptev[q][0], ptev[q][1])
                    W("tensor", p2f[gs][0], p2f[gs][1])
                    avd = 0
                    for kt in range(16):
                        avd = OP("tensor", lambda t, _p=P2[gs][:],
                                 _v=uni[:, U_V + kt * 256 + h * 128:U_V + kt * 256 + (h + 1) * 128],
                                 _pt=uni[:, U_PT + kt * 512:U_PT + (kt + 1) * 512],
                                 st=(kt == 0), sp=(kt == 15): t.matmul(_p, _v, _pt, start=st, stop=sp),
                                 incs="pe" if kt == 15 else None) or avd
                    ev = OP("scalar", lambda s, o=ao[:, h * 2048 + g4 * 512:h * 2048 + (g4 + 1) * 512],
                            i2=P2[gs][:]: s.copy(out=o, in_=i2), incs="act",
                            waits=[("pe", avd)])
                    p2f[gs] = ("act", ev)
                    ao_ev[0] = ev
            p1f[0] = [("act", prev["exp"]), ("dve", prev["nm"])]
            for k in range(1, 4):
                p1f[k] = [("act", prev["exp"])]
            return avd

        def emit_B5(c, avd, dec_per_j=0):
            first = True
            ld = {}
            for j in range(16):
                if dec_per_j:
                    emit_decode(dec_per_j if j < 14 else 1)
                for wi, w in enumerate(("g", "u")):
                    slot = wi * 2 + j % 2
                    ld[(w, j)] = DMA("sync", "ld_sp",
                                     wpool[:, slot * 2048:(slot + 1) * 2048], dec[w][j],
                                     waits=[("ld_act", dec_store[(w, j)]), wslot_use[slot]])
                for tg in range(4):
                    qb = tg % 2
                    bg = P1[:, (2 * qb) * 512:(2 * qb + 1) * 512]
                    bu = P1[:, (2 * qb + 1) * 512:(2 * qb + 2) * 512]
                    W("tensor", "ld_sp", max(ld[("g", j)], ld[("u", j)], xnt_ld[c]))
                    for bb in p1f[2 * qb] + p1f[2 * qb + 1]:
                        W("tensor", bb[0], bb[1])
                    gmm = umm = 0
                    for kd in range(16):
                        gmm = OP("tensor", lambda t, _p=bg, _w=wpool[:, (j % 2) * 2048 + kd * 128:
                                 (j % 2) * 2048 + (kd + 1) * 128],
                                 _x=xnt[:, kd * 2048 + tg * 512:kd * 2048 + (tg + 1) * 512],
                                 st=(kd == 0), sp=(kd == 15): t.matmul(_p, _w, _x, start=st, stop=sp),
                                 incs="pe" if kd == 15 else None) or gmm
                    for kd in range(16):
                        umm = OP("tensor", lambda t, _p=bu, _w=wpool[:, (2 + j % 2) * 2048 + kd * 128:
                                 (2 + j % 2) * 2048 + (kd + 1) * 128],
                                 _x=xnt[:, kd * 2048 + tg * 512:kd * 2048 + (tg + 1) * 512],
                                 st=(kd == 0), sp=(kd == 15): t.matmul(_p, _w, _x, start=st, stop=sp),
                                 incs="pe" if kd == 15 else None) or umm
                    if tg == 3:
                        wslot_use[j % 2] = ("pe", gmm)
                        wslot_use[2 + j % 2] = ("pe", umm)
                    geld = OP("scalar", lambda s, _g=bg, o=gg[:, qb * 512:(qb + 1) * 512]:
                              s.activation(out=o, in_=_g, func=AF.Gelu_apprx_tanh),
                              incs="act", waits=[("pe", gmm), gg_free[qb]])
                    p1f[2 * qb] = [("act", geld)]
                    iw = [("pe", umm), ("act", geld)]
                    if first:
                        iw += [("pe", avd), ("act", ao_ev[0])]
                        first = False
                    intd = OP("vector", lambda v, _u=bu,
                              o=uni[:, j * 2048 + tg * 512:j * 2048 + (tg + 1) * 512],
                              g2=gg[:, qb * 512:(qb + 1) * 512]:
                              v.tensor_tensor(o, g2, _u, A.mult), incs="dve", waits=iw)
                    p1f[2 * qb + 1] = [("dve", intd)]
                    gg_free[qb] = ("dve", intd)
            b5_last[c] = cnt["pe"]
            b5_intd[0] = cnt["dve"]

        def emit_B6(c):
            part_st = 0
            rs_half = [0, 0]
            W("tensor", "act", ao_ev[0])
            W("tensor", "dve", b5_intd[0])
            if c == 0:
                W("tensor", "dve", a1_all)
            wo_ld = 0
            for kt in range(2):
                wo_ld = DMA("sync", "ld_sp", wpool[:, (2 + kt) * 2048:(3 + kt) * 2048],
                            dec_a[6 + kt], waits=[("ld_act", a1_prev_st), wslot_use[2 + kt]])
            for idx, dtl in enumerate(range(16)):
                slot = idx % 2
                ldd = DMA("sync", "ld_sp", wpool[:, slot * 2048:(slot + 1) * 2048],
                          dec["d"][dtl],
                          waits=[("ld_act", dec_store[("d", dtl)]), wslot_use[slot]])
                for tg in range(4):
                    gs = (4 * dtl + tg) % 2
                    pp = P2[gs]
                    W("tensor", "ld_sp", max(ldd, wo_ld))
                    W("tensor", p2f[gs][0], p2f[gs][1])
                    for kt in range(2):
                        OP("tensor", lambda t, _p=pp[:],
                           _w=wpool[:, (2 + kt) * 2048 + dtl * 128:(2 + kt) * 2048 + (dtl + 1) * 128],
                           _a=ao[:, kt * 2048 + tg * 512:kt * 2048 + (tg + 1) * 512],
                           st=(kt == 0): t.matmul(_p, _w, _a, start=st, stop=False))
                    pdn = 0
                    for kn in range(16):
                        pdn = OP("tensor", lambda t, _p=pp[:],
                                 _w=wpool[:, slot * 2048 + kn * 128:slot * 2048 + (kn + 1) * 128],
                                 _i=uni[:, kn * 2048 + tg * 512:kn * 2048 + (tg + 1) * 512],
                                 sp=(kn == 15): t.matmul(_p, _w, _i, start=False, stop=sp),
                                 incs="pe" if kn == 15 else None) or pdn
                    if tg == 3:
                        wslot_use[slot] = ("pe", pdn)
                        if dtl == 15:
                            wslot_use[2] = ("pe", pdn)
                            wslot_use[3] = ("pe", pdn)
                    ypc = OP("scalar", lambda s, o=yp[:, gs * 512:(gs + 1) * 512], i2=pp[:]:
                             s.copy(out=o, in_=i2), incs="act",
                             waits=[("pe", pdn), yp_free[gs]])
                    p2f[gs] = ("act", ypc)
                    part_st = DMA("scalar", "ld_act",
                                  part[c][dtl * 128:(dtl + 1) * 128, tg * 512:(tg + 1) * 512],
                                  yp[:, gs * 512:(gs + 1) * 512], waits=[("act", ypc)])
                    yp_free[gs] = ("ld_act", part_st)
            W("gpsimd", "ld_act", part_st)
            rs_half[0] = rs_half[1] = OP("gpsimd", lambda g, _c=c: g.collective_compute(
                "ReduceScatter", A.add, replica_groups=[list(range(R))],
                ins=[part[_c].opt()], outs=[rs_o[_c].opt()]), incs="cc")
            rs_done[c] = rs_half

        def emit_F(c):
            seen = set()
            stv = 0
            for pc in range(8):
                dtl, tg = pc // 4, pc % 4
                slot = pc % 2
                rsb = fnb[:, slot * 1536:slot * 1536 + 512]
                xtb = fnb[:, slot * 1536 + 512:slot * 1536 + 1024]
                yb = fnb[:, slot * 1536 + 1024:slot * 1536 + 1536]
                firstw = [("cc", rs_done[c][dtl])] if dtl not in seen else []
                seen.add(dtl)
                DMA("gpsimd", "ld_pool", rsb,
                    rs_o[c][dtl * 128:(dtl + 1) * 128, tg * 512:(tg + 1) * 512],
                    waits=firstw + [fn_free[slot]])
                l2 = DMA("gpsimd", "ld_pool", xtb,
                         xT_s[dtl * 128:(dtl + 1) * 128, c * 2048 + tg * 512:c * 2048 + (tg + 1) * 512])
                yd = OP("vector", lambda v, _r=rsb, _x=xtb, _y=yb, bi=bo_sb[:, dtl:dtl + 1]:
                        v.scalar_tensor_tensor(_y, _r, bi, _x, A.add, A.add),
                        incs="dve", waits=[("ld_pool", l2)])
                stv = DMA("gpsimd", "ld_pool",
                          yT_s[dtl * 128:(dtl + 1) * 128, c * 2048 + tg * 512:c * 2048 + (tg + 1) * 512],
                          yb, waits=[("dve", yd)])
                fn_free[slot] = ("dve", yd)
            return stv

        emit_decode(4)
        emit_xnt_load(0)
        emit_B2(0)
        emit_decode(8)
        emit_B3(0)
        avd = emit_B4(0, 0, 0)
        emit_decode(8)
        avd = emit_B4(0, 1, avd)
        emit_decode(12)
        emit_B5(0, avd)
        emit_decode(len(dec_order))
        emit_B6(0)
        for c in range(1, 4):
            emit_xnt_load(c)
            emit_B2(c)
            emit_B3(c)
            avd = emit_B4(c, 0, avd)
            avd = emit_B4(c, 1, avd)
            emit_B5(c, avd)
            emit_B6(c)
            emit_F(c - 1)
        last_st = emit_F(3)
        W("gpsimd", "ld_pool", last_st)

        with nc.Block() as block:
            @block.sync
            def _(e):
                for f in ops["sync"]:
                    f(e)

            @block.vector
            def _(e):
                for f in ops["vector"]:
                    f(e)

            @block.scalar
            def _(e):
                for f in ops["scalar"]:
                    f(e)

            @block.tensor
            def _(e):
                for f in ops["tensor"]:
                    f(e)

            @block.gpsimd
            def _(e):
                for f in ops["gpsimd"]:
                    f(e)
    return nc


_CACHE = {}


def kernel(L0_attn_x, L0_attn_wq_codes, L0_attn_wq_scales, L0_attn_bq,
           L0_attn_wk_codes, L0_attn_wk_scales, L0_attn_bk,
           L0_attn_wv_codes, L0_attn_wv_scales, L0_attn_bv,
           L0_attn_wo_codes, L0_attn_wo_scales, L0_attn_bo,
           L0_ffn_gate_W_packed, L0_ffn_gate_scales,
           L0_ffn_up_W_packed, L0_ffn_up_scales,
           L0_ffn_down_W_packed, L0_ffn_down_scales):
    if "nc" not in _CACHE:
        _CACHE["nc"] = build_nc()
    nc = _CACHE["nc"]

    ac = np.ascontiguousarray
    xf = np.asarray(L0_attn_x, dtype=np.float32).reshape(T, D)
    xT = ac(xf.T)
    rep = np.repeat(np.arange(64), 32)
    wq_se_f = np.asarray(L0_attn_wq_scales)[rep, :]
    wk_se_f = np.asarray(L0_attn_wk_scales)[rep, :]
    wv_se_f = np.asarray(L0_attn_wv_scales)[rep, :]
    wo_se_f = np.asarray(L0_attn_wo_scales)[rep, :]
    ident = np.eye(128, dtype=ml_dtypes.bfloat16)
    bq = np.asarray(L0_attn_bq)
    bk = np.asarray(L0_attn_bk)
    bv = np.asarray(L0_attn_bv)
    bo = np.asarray(L0_attn_bo)

    in_maps = []
    for c in range(R):
        f0 = FT * c
        n0 = 2048 * c
        m = {
            "x_tok": ac(xf[TKC * c:TKC * (c + 1)]),
            "xT_s": ac(xT[f0:f0 + FT]),
            "wq_c": ac(np.asarray(L0_attn_wq_codes)[:, f0:f0 + FT]),
            "wk_c": ac(np.asarray(L0_attn_wk_codes)[:, f0:f0 + FT]),
            "wv_c": ac(np.asarray(L0_attn_wv_codes)[:, f0:f0 + FT]),
            "wq_se": ac(wq_se_f[:, f0:f0 + FT]),
            "wk_se": ac(wk_se_f[:, f0:f0 + FT]),
            "wv_se": ac(wv_se_f[:, f0:f0 + FT]),
            "wo_c": ac(np.asarray(L0_attn_wo_codes)[f0:f0 + FT, :]),
            "wo_se": ac(wo_se_f[f0:f0 + FT, :]),
            "bqkv": ac(np.stack([bq[f0:f0 + 128], bq[f0 + 128:f0 + 256],
                                 bk[f0:f0 + 128], bk[f0 + 128:f0 + 256],
                                 bv[f0:f0 + 128], bv[f0 + 128:f0 + 256]], axis=1)),
            "bo_s": ac(np.stack([bo[f0:f0 + 128], bo[f0 + 128:f0 + 256]], axis=1)),
            "g_p": ac(np.asarray(L0_ffn_gate_W_packed)[n0:n0 + 2048]),
            "g_s": ac(np.asarray(L0_ffn_gate_scales)[n0:n0 + 2048]),
            "u_p": ac(np.asarray(L0_ffn_up_W_packed)[n0:n0 + 2048]),
            "u_s": ac(np.asarray(L0_ffn_up_scales)[n0:n0 + 2048]),
            "d_p": ac(np.asarray(L0_ffn_down_W_packed)[:, 1024 * c:1024 * (c + 1)]),
            "d_s": ac(np.asarray(L0_ffn_down_scales)[:, 64 * c:64 * (c + 1)]),
            "ident": ident,
        }
        in_maps.append(m)

    res = run_bass_kernel_spmd(nc, in_maps, core_ids=list(range(R)))
    shards = [np.asarray(res.results[c]["yT_s"]) for c in range(R)]
    yT = np.concatenate(shards, axis=0)
    return ac(yT.T).reshape(B_, S, D)


# version 17
# speedup vs baseline: 1.0227x; 1.0227x over previous
import numpy as np
import ml_dtypes
from contextlib import ExitStack
import concourse.bass as bass
import concourse.mybir as mybir
from concourse.bass_utils import run_bass_kernel_spmd

F32 = mybir.dt.float32
BF16 = mybir.dt.bfloat16
I32 = mybir.dt.int32
U16 = mybir.dt.uint16
A = mybir.AluOpType
AF = mybir.ActivationFunctionType
AX = mybir.AxisListType

R = 8
B_, S, D = 4, 2048, 2048
T = B_ * S
TKC = T // R
FT = 256
ISQ = 0.08838834764831845
NQS = 4


def build_nc():
    nc = bass.Bass("TRN2", target_bir_lowering=False)

    def din(name, shape, dt):
        return nc.dram_tensor(name, shape, dt, kind="ExternalInput").ap()

    def dint(name, shape, dt):
        return nc.dram_tensor(name, shape, dt, kind="Internal").ap()

    x_tok = din("x_tok", [TKC, D], F32)
    xT_s = din("xT_s", [FT, T], F32)
    w_qkv_c = [din(f"w{p}_c", [D, FT], F32) for p in "qkv"]
    w_qkv_se = [din(f"w{p}_se", [D, FT], F32) for p in "qkv"]
    wo_c = din("wo_c", [FT, D], F32)
    wo_se = din("wo_se", [FT, D], F32)
    bqkv_h = din("bqkv", [128, 6], F32)
    bo_h = din("bo_s", [128, 2], F32)
    ffn_p = {w: din(f"{w}_p", [2048, 1024], I32) for w in ("g", "u", "d")}
    ffn_s = {w: din(f"{w}_s", [2048, 64], F32) for w in ("g", "u", "d")}
    ident_h = din("ident", [128, 128], BF16)
    yT_s = nc.dram_tensor("yT_s", [FT, T], F32, kind="ExternalOutput").ap()

    ag_in = dint("ag_in", [D, TKC], BF16)
    ag_out = [dint(f"ag_out{h}", [R * 1024, TKC], BF16) for h in range(2)]
    dec = {w: dint(f"dec_{w}", [16, 128, 2048], BF16) for w in ("g", "u", "d")}
    dec_a = dint("dec_a", [8, 128, 2048], BF16)
    part = [dint(f"part{c}", [D, S], F32) for c in range(4)]
    rs_o = [dint(f"rs_o{c}", [FT, S], F32) for c in range(4)]

    with ExitStack() as ctx:
        def sbuf(name, shape, dt):
            return ctx.enter_context(nc.sbuf_tensor(name, shape, dt))

        def psum(name, shape, dt):
            return ctx.enter_context(nc.psum_tensor(name, shape, dt))

        xnt = sbuf("xnt", [128, 32768], BF16)
        uni = sbuf("uni", [128, 32768], BF16)
        ao = sbuf("ao", [128, 4096], BF16)
        wpool = sbuf("wpool", [128, 8192], BF16)
        apb = sbuf("apb", [128, 4096], BF16)
        pk = sbuf("pk", [128, 2048], I32)
        dtmp = sbuf("dtmp", [128, 4096], I32)
        codes = sbuf("codes", [128, 2048], U16)
        dq = sbuf("dq", [128, 2048], BF16)
        stage = sbuf("stage", [128, 2048], BF16)
        scb = sbuf("scb", [128, 128], F32)
        id_sb = sbuf("id_sb", [128, 128], BF16)
        bqkv_sb = sbuf("bqkv_sb", [128, 6], F32)
        bo_sb = sbuf("bo_sb", [128, 2], F32)
        gg = sbuf("gg", [128, 1024], BF16)
        yp = sbuf("yp", [128, 1024], F32)
        smal = sbuf("smal", [128, 16], F32)
        xn_t = stage

        P1 = psum("P1", [128, 2048], F32)
        P2 = [psum("P2a", [128, 512], F32), psum("P2b", [128, 512], F32)]
        P3 = [psum("P3a", [128, 1024], BF16), psum("P3b", [128, 1024], BF16)]

        qsems = {q: [f"{q}{i}" for i in range(NQS)] for q in ("sync", "scalar", "gpsimd")}
        semnames = ["cc", "pe", "dve", "act"] + sum(qsems.values(), [])
        sem = {n: ctx.enter_context(nc.semaphore(n)) for n in semnames}
        cnt = {n: 0 for n in semnames}
        qrot = {q: 0 for q in qsems}
        ops = {"sync": [], "vector": [], "scalar": [], "tensor": [], "gpsimd": []}

        def em(eng, fn):
            ops[eng].append(fn)

        def W(eng, s, v):
            if v > 0:
                em(eng, lambda e, _s=sem[s], _v=v: e.wait_ge(_s, _v))

        def Wl(eng, toks):
            for (s, v) in toks:
                W(eng, s, v)

        def inc(s, n=1):
            cnt[s] += n
            return cnt[s]

        def DMA(q, out_ap, in_ap, waits=()):
            Wl(q, waits)
            sname = qsems[q][qrot[q] % NQS]
            qrot[q] += 1
            c = inc(sname, 16)
            em(q, lambda e, o=out_ap, i=in_ap, sm=sem[sname]:
               e.dma_start(out=o, in_=i).then_inc(sm, 16))
            return (sname, c)

        def qmark(q):
            return [(s, cnt[s]) for s in qsems[q] if cnt[s] > 0]

        def OP(eng, build, incs=None, waits=()):
            Wl(eng, waits)
            if incs:
                c = inc(incs)
                em(eng, lambda e, b=build, sm=sem[incs]: b(e).then_inc(sm, 1))
                return (incs, c)
            em(eng, lambda e, b=build: b(e))
            return None

        u_f32 = uni[:].bitcast(F32)
        wp_f32 = wpool[:].bitcast(F32)
        jnk = codes[:].bitcast(BF16)
        fnb = dtmp[:].bitcast(F32)

        U_QT, U_KT, U_V, U_P, U_PN, U_VT, U_PT = 0, 4096, 8192, 12288, 14336, 16384, 16384
        Z = ("pe", 0)

        p1f = [[Z], [Z], [Z], [Z]]
        p2f = [Z, Z]
        p3f = [Z, Z]

        for o, i in ((id_sb, ident_h), (bqkv_sb, bqkv_h), (bo_sb, bo_h)):
            DMA("sync", o[:], i)
        con_ld = qmark("sync")

        OP("vector", lambda v: v.memset(smal[:, 10:11], 1e-5))
        OP("vector", lambda v: v.memset(smal[:, 11:12], 16128.0), incs="dve")
        ln_done = []
        pe_tr = Z
        for tt in range(8):
            slot = tt % 2
            xsl = wp_f32[:, slot * 2048:(slot + 1) * 2048]
            ldw = [ln_done[tt - 2]] if tt >= 2 else ()
            ldv = DMA("sync", xsl, x_tok[tt * 128:(tt + 1) * 128, :], waits=ldw)
            rs1 = OP("vector", lambda v, x=xsl: v.reduce_sum(smal[:, 0:1], x, axis=AX.X),
                     incs="dve", waits=[ldv])
            nmn = OP("vector", lambda v: v.tensor_single_scalar(smal[:, 2:3], smal[:, 0:1],
                     -1.0 / D, A.mult), incs="dve", waits=[rs1])
            sq = OP("scalar", lambda s, x=xsl: s.activation(out=jnk[:], in_=x, func=AF.Square,
                    accum_out=smal[:, 1:2]), incs="act", waits=[ldv])
            OP("vector", lambda v: v.tensor_single_scalar(smal[:, 3:4], smal[:, 1:2],
                                                          1.0 / D, A.mult), waits=[sq, nmn])
            m2c = OP("vector", lambda v: v.tensor_tensor(smal[:, 4:5], smal[:, 2:3],
                     smal[:, 2:3], A.mult), incs="dve")
            vd = OP("vector", lambda v: v.tensor_tensor(smal[:, 12:13], smal[:, 3:4],
                    smal[:, 4:5], A.subtract), incs="dve", waits=[m2c])
            sdc = OP("scalar", lambda s: s.activation(out=smal[:, 13:14], in_=smal[:, 12:13],
                     func=AF.Sqrt, bias=smal[:, 10:11], scale=1.0), incs="act", waits=[vd])
            rcc = OP("vector", lambda v: v.reciprocal(smal[:, 5:6], smal[:, 13:14]),
                     incs="dve", waits=[sdc])
            xw = [rcc] + ([pe_tr] if tt >= 1 else [])
            xnd = OP("vector", lambda v, x=xsl: v.tensor_scalar(xn_t[:], x, smal[:, 2:3],
                     smal[:, 5:6], A.add, A.mult), incs="dve", waits=xw)
            ln_done.append(xnd)
            Wl("tensor", [xnd])
            if tt == 0:
                Wl("tensor", con_ld)
            for half in range(2):
                pd = P3[half]
                Wl("tensor", [p3f[half]])
                for k in range(8):
                    dtl = half * 8 + k
                    pe_tr = OP("tensor", lambda t, p=pd[:, k * 128:(k + 1) * 128],
                               i=xn_t[:, dtl * 128:(dtl + 1) * 128]:
                               t.transpose(p, i, id_sb[:]),
                               incs="pe" if k == 7 else None) or pe_tr
                dst = uni[:].rearrange("p (dt q) -> p dt q", q=1024)[
                    :, half * 8:half * 8 + 8, tt * 128:(tt + 1) * 128]
                ev = OP("scalar", lambda s, o=dst,
                        i=pd[:].rearrange("p (dt q) -> p dt q", q=128):
                        s.copy(out=o, in_=i), incs="act", waits=[pe_tr])
                p3f[half] = ev
        W("gpsimd", "act", cnt["act"])
        for dtl in range(16):
            DMA("gpsimd", ag_in[dtl * 128:(dtl + 1) * 128, :],
                uni[:, dtl * 1024:(dtl + 1) * 1024])
        ag_st = qmark("gpsimd")
        Wl("gpsimd", ag_st)
        ag_done = [0, 0]
        for h in range(2):
            ag_done[h] = OP("gpsimd", lambda g, _h=h: g.collective_compute(
                "AllGather", A.bypass, replica_groups=[list(range(R))],
                ins=[ag_in[1024 * _h:1024 * (_h + 1), :].opt()],
                outs=[ag_out[_h].opt()]), incs="cc")

        st_free2 = [("act", 0), ("act", 0)]
        a1_srcs = [(w_qkv_c[0], w_qkv_se[0]), (w_qkv_c[1], w_qkv_se[1]),
                   (w_qkv_c[2], w_qkv_se[2]), (wo_c, wo_se)]
        a1_ttp = [None] * 4
        sidx = 0
        for pi in range(4):
            ch, sh = a1_srcs[pi]
            base = (pi % 2) * 8192
            lw = list(ag_st) if pi < 2 else [a1_ttp[pi - 2]]
            toks = []
            if pi < 3:
                cin = ch[:].rearrange("(dt p) f -> p dt f", p=128)
                sin = sh[:].rearrange("(dt p) f -> p dt f", p=128)
                toks.append(DMA("scalar",
                                u_f32[:, base:base + 4096].rearrange("p (dt f) -> p dt f", f=FT),
                                cin, waits=lw))
                toks.append(DMA("scalar",
                                u_f32[:, base + 4096:base + 8192].rearrange("p (dt f) -> p dt f", f=FT),
                                sin))
            else:
                toks.append(DMA("scalar", u_f32[:, base:base + 2048], ch[0:128, :], waits=lw))
                toks.append(DMA("scalar", u_f32[:, base + 2048:base + 4096], ch[128:256, :]))
                toks.append(DMA("scalar", u_f32[:, base + 4096:base + 6144], sh[0:128, :]))
                toks.append(DMA("scalar", u_f32[:, base + 6144:base + 8192], sh[128:256, :]))
            Wl("vector", toks)
            a1_tt = None
            for f in range(2):
                for hg in range(2):
                    ss = sidx % 2
                    sidx += 1
                    Wl("vector", [st_free2[ss]])
                    for k in range(8):
                        dtl = hg * 8 + k
                        if pi < 3:
                            a_ = u_f32[:, base + dtl * 256 + f * 128:base + dtl * 256 + f * 128 + 128]
                            b_ = u_f32[:, base + 4096 + dtl * 256 + f * 128:base + 4096 + dtl * 256 + f * 128 + 128]
                        else:
                            a_ = u_f32[:, base + f * 2048 + dtl * 128:base + f * 2048 + (dtl + 1) * 128]
                            b_ = u_f32[:, base + 4096 + f * 2048 + dtl * 128:base + 4096 + f * 2048 + (dtl + 1) * 128]
                        o = stage[:, ss * 1024 + k * 128:ss * 1024 + (k + 1) * 128]
                        a1_tt = OP("vector", lambda v, _o=o, _a=a_, _b=b_:
                                   v.tensor_tensor(_o, _a, _b, A.mult),
                                   incs="dve" if k == 7 else None) or a1_tt
                    stv = DMA("scalar", dec_a[2 * pi + f][:, hg * 1024:(hg + 1) * 1024],
                              stage[:, ss * 1024:(ss + 1) * 1024], waits=[a1_tt])
                    st_free2[ss] = stv
            a1_ttp[pi] = a1_tt
        a1_all = cnt["dve"]
        a1_st = qmark("scalar")

        Tt = [dtmp[:, i * 1024:(i + 1) * 1024] for i in range(4)]
        pk_free = [("dve", 0), ("dve", 0)]
        sc_free = [("dve", 0), ("dve", 0)]
        st_free = st_free2
        dec_store = {}
        dec_i = [0]
        dec_order = [(w, rt) for rt in range(16) for w in ("g", "u")] + \
                    [("d", rt) for rt in range(16)]
        p0_act = cnt["act"]

        def emit_decode(n):
            for _ in range(n):
                if dec_i[0] >= len(dec_order):
                    return
                i = dec_i[0]
                dec_i[0] += 1
                w, rt = dec_order[i]
                slot = i % 2
                pks = pk[:, slot * 1024:(slot + 1) * 1024]
                scs = scb[:, slot * 64:(slot + 1) * 64]
                pkv = DMA("gpsimd", pks, ffn_p[w][rt * 128:(rt + 1) * 128, :],
                          waits=[pk_free[slot]])
                scv = DMA("gpsimd", scs, ffn_s[w][rt * 128:(rt + 1) * 128, :],
                          waits=[sc_free[slot]])
                w0 = [pkv] + ([("act", p0_act)] if i == 0 else [])
                dcd = None
                for lohi in range(2):
                    if lohi == 0:
                        OP("vector", lambda v, p=pks: v.tensor_single_scalar(
                           Tt[0][:], p, 7, A.bitwise_and), waits=w0)
                        OP("vector", lambda v, p=pks: v.tensor_single_scalar(
                           Tt[1][:], p, 8, A.bitwise_and))
                    else:
                        OP("vector", lambda v, p=pks: v.tensor_scalar(
                           Tt[0][:], p, 4, 7, A.logical_shift_right, A.bitwise_and))
                        pkfree = OP("vector", lambda v, p=pks: v.tensor_scalar(
                                    Tt[1][:], p, 4, 8, A.logical_shift_right, A.bitwise_and),
                                    incs="dve")
                        pk_free[slot] = pkfree
                    bbv = OP("vector", lambda v: v.tensor_scalar(Tt[2][:], Tt[0][:],
                             64, 16128, A.mult, A.add), incs="dve")
                    mmn = OP("vector", lambda v: v.scalar_tensor_tensor(Tt[3][:], Tt[0][:],
                             16128.0, Tt[2][:], A.mult, A.min), incs="dve", waits=[bbv])
                    dst = codes[:, lohi:2048:2]
                    dcd = OP("vector", lambda v, _d=dst: v.scalar_tensor_tensor(
                             _d, Tt[1][:], 4096.0, Tt[3][:], A.mult, A.add),
                             incs="dve", waits=[mmn])
                dqd = OP("vector", lambda v, _s=scs:
                         v.tensor_tensor(dq[:].rearrange("p (j r) -> p j r", r=32),
                                         codes[:].bitcast(BF16).rearrange("p (j r) -> p j r", r=32),
                                         _s.unsqueeze(2).broadcast_to([128, 64, 32]),
                                         A.mult), incs="dve", waits=[scv, dcd])
                sc_free[slot] = dqd
                Wl("tensor", [dqd])
                for half in range(2):
                    pd = P3[half]
                    Wl("tensor", [p3f[half]])
                    ptr = None
                    for k in range(8):
                        kt2 = half * 8 + k
                        ptr = OP("tensor", lambda t, p=pd[:, k * 128:(k + 1) * 128],
                                 i2=dq[:, kt2 * 128:(kt2 + 1) * 128]:
                                 t.transpose(p, i2, id_sb[:]),
                                 incs="pe" if k == 7 else None) or ptr
                    ss2 = (2 * i + half) % 2
                    ev = OP("scalar", lambda s, o=stage[:, ss2 * 1024:(ss2 + 1) * 1024],
                            i2=pd[:]: s.copy(out=o, in_=i2), incs="act",
                            waits=[ptr, st_free[ss2]])
                    p3f[half] = ev
                    stv = DMA("scalar", dec[w][rt][:, half * 1024:(half + 1) * 1024],
                              stage[:, ss2 * 1024:(ss2 + 1) * 1024], waits=[ev])
                    st_free[ss2] = stv
                dec_store[(w, rt)] = stv

        xnt_h0 = [None] * 4
        xnt_all = [None] * 4
        apb_use = [Z, Z]
        wslot_use = [Z, Z, Z, Z]
        yp_free = [("act", 0), ("act", 0)]
        b5_last = [0, 0, 0, 0]
        rs_done = [0, 0, 0, 0]
        fn_free = [("dve", 0), ("dve", 0)]
        gg_free = [("dve", 0), ("dve", 0)]
        qk_ev = [Z]
        ao_ev = [Z]
        b5_intd = [0]

        def emit_xnt_load(c):
            w0 = [("pe", b5_last[c - 1])] if c > 0 else []
            for half in range(2):
                waits = w0 + ([("cc", ag_done[half][1])] if c == 0 else [])
                w0 = []
                for dtl in range(8 * half, 8 * half + 8):
                    for h2 in range(2):
                        DMA("sync",
                            xnt[:, dtl * 2048 + h2 * 1024:dtl * 2048 + (h2 + 1) * 1024],
                            ag_out[half][(2 * c + h2) * 1024 + (dtl - 8 * half) * 128:
                                         (2 * c + h2) * 1024 + (dtl - 8 * half + 1) * 128, :],
                            waits=waits)
                        waits = ()
                if half == 0:
                    xnt_h0[c] = qmark("sync")
            xnt_all[c] = qmark("sync")

        def emit_B2(c):
            last_ev = Z
            for ft in range(6):
                slot = ft % 2
                apb_ld = DMA("sync", apb[:, slot * 2048:(slot + 1) * 2048],
                             dec_a[ft], waits=a1_st + [apb_use[slot]])
                for tg in range(4):
                    g = (4 * ft + tg) % 2
                    pp = P2[g]
                    Wl("tensor", [apb_ld] + xnt_h0[c])
                    Wl("tensor", [p2f[g]])
                    cp = None
                    for dtl in range(16):
                        if dtl == 8:
                            Wl("tensor", xnt_all[c])
                        cp = OP("tensor", lambda t, _p=pp[:], _w=apb[:, slot * 2048 + dtl * 128:
                                slot * 2048 + (dtl + 1) * 128],
                                _x=xnt[:, dtl * 2048 + tg * 512:dtl * 2048 + (tg + 1) * 512],
                                st=(dtl == 0), sp=(dtl == 15): t.matmul(_p, _w, _x, start=st, stop=sp),
                                incs="pe" if dtl == 15 else None) or cp
                    if tg == 3:
                        apb_use[slot] = cp
                    if ft < 2:
                        dst = uni[:, U_QT + ft * 2048 + tg * 512:U_QT + ft * 2048 + (tg + 1) * 512]
                    elif ft < 4:
                        dst = uni[:, U_KT + (ft - 2) * 2048 + tg * 512:U_KT + (ft - 2) * 2048 + (tg + 1) * 512]
                    else:
                        dst = uni[:, U_VT + (ft - 4) * 2048 + tg * 512:U_VT + (ft - 4) * 2048 + (tg + 1) * 512]
                    ev = OP("scalar", lambda s, o=dst, i2=pp[:], bi=bqkv_sb[:, ft:ft + 1]:
                            s.activation(out=o, in_=i2, func=AF.Identity, bias=bi),
                            incs="act", waits=[cp])
                    p2f[g] = ev
                    last_ev = ev
            qk_ev[0] = last_ev

        def emit_B3(c):
            Wl("tensor", [qk_ev[0]])
            for h in range(2):
                for half in range(2):
                    pd = P3[half]
                    Wl("tensor", [p3f[half]])
                    ptr = None
                    for k in range(8):
                        tk = half * 8 + k
                        ptr = OP("tensor", lambda t, p=pd[:, k * 128:(k + 1) * 128],
                                 i2=uni[:, U_VT + h * 2048 + tk * 128:U_VT + h * 2048 + (tk + 1) * 128]:
                                 t.transpose(p, i2, id_sb[:]),
                                 incs="pe" if k == 7 else None) or ptr
                    dstv = uni[:, U_V + half * 2048:U_V + (half + 1) * 2048].rearrange(
                        "p (a r) -> p a r", r=256)[:, :, h * 128:(h + 1) * 128]
                    ev = OP("scalar", lambda s, o=dstv,
                            i2=pd[:].rearrange("p (a r) -> p a r", r=128):
                            s.copy(out=o, in_=i2), incs="act", waits=[ptr])
                    p3f[half] = ev

        def emit_B4(c, h, avd):
            prev_exp = prev_nm = Z
            ptev = Z
            for q in range(16):
                if q == 0:
                    Wl("tensor", [qk_ev[0]])
                    for k in range(4):
                        Wl("tensor", p1f[k])
                else:
                    Wl("tensor", [prev_exp, prev_nm])
                sc0 = sc3 = None
                for kt in range(4):
                    r_ = OP("tensor", lambda t, _p=P1[:, kt * 512:(kt + 1) * 512],
                            _q=uni[:, U_QT + h * 2048 + q * 128:U_QT + h * 2048 + (q + 1) * 128],
                            _k=uni[:, U_KT + h * 2048 + kt * 512:U_KT + h * 2048 + (kt + 1) * 512]:
                            t.matmul(_p, _q, _k, start=True, stop=True),
                            incs="pe" if kt in (0, 3) else None)
                    if kt == 0:
                        sc0 = r_
                    elif kt == 3:
                        sc3 = r_
                rmx = OP("vector", lambda v: v.reduce_max(smal[:, 6:7], P1[:, 0:512], axis=AX.X,
                         negate=True), incs="dve", waits=[sc0, prev_exp])
                nm = OP("vector", lambda v: v.tensor_single_scalar(smal[:, 7:8], smal[:, 6:7],
                        ISQ, A.mult), incs="dve", waits=[rmx])
                expd = OP("scalar", lambda s: s.activation(out=uni[:, U_P:U_P + 2048],
                          in_=P1[:], func=AF.Exp, scale=ISQ, bias=smal[:, 7:8],
                          accum_out=smal[:, 8:9]), incs="act", waits=[sc3, nm])
                rcp = OP("vector", lambda v: v.reciprocal(smal[:, 9:10], smal[:, 8:9]),
                         incs="dve", waits=[expd])
                pn = OP("vector", lambda v: v.tensor_single_scalar(uni[:, U_PN:U_PN + 2048],
                        uni[:, U_P:U_P + 2048], smal[:, 9:10], A.mult), incs="dve",
                        waits=[rcp])
                prev_exp, prev_nm = expd, nm
                ql = q % 4
                Wl("tensor", [pn])
                for half in range(2):
                    pd = P3[half]
                    Wl("tensor", [p3f[half]])
                    ptr = None
                    for k in range(8):
                        kt = half * 8 + k
                        ptr = OP("tensor", lambda t, p=pd[:, k * 128:(k + 1) * 128],
                                 i2=uni[:, U_PN + kt * 128:U_PN + (kt + 1) * 128]:
                                 t.transpose(p, i2, id_sb[:]),
                                 incs="pe" if k == 7 else None) or ptr
                    dstv = uni[:, U_PT + half * 8 * 512:U_PT + (half * 8 + 8) * 512].rearrange(
                        "p (a r) -> p a r", r=512)[:, :, ql * 128:(ql + 1) * 128]
                    ev = OP("scalar", lambda s, o=dstv,
                            i2=pd[:].rearrange("p (a r) -> p a r", r=128):
                            s.copy(out=o, in_=i2), incs="act",
                            waits=[ptr] + ([avd] if half == 0 else []))
                    p3f[half] = ev
                    ptev = ev
                if ql == 3:
                    g4 = q // 4
                    gs = g4 % 2
                    Wl("tensor", [ptev, p2f[gs]])
                    for kt in range(16):
                        r_ = OP("tensor", lambda t, _p=P2[gs][:],
                                _v=uni[:, U_V + kt * 256 + h * 128:U_V + kt * 256 + (h + 1) * 128],
                                _pt=uni[:, U_PT + kt * 512:U_PT + (kt + 1) * 512],
                                st=(kt == 0), sp=(kt == 15): t.matmul(_p, _v, _pt, start=st, stop=sp),
                                incs="pe" if kt == 15 else None)
                        avd = r_ or avd
                    ev = OP("scalar", lambda s, o=ao[:, h * 2048 + g4 * 512:h * 2048 + (g4 + 1) * 512],
                            i2=P2[gs][:]: s.copy(out=o, in_=i2), incs="act", waits=[avd])
                    p2f[gs] = ev
                    ao_ev[0] = ev
            p1f[0] = [prev_exp, prev_nm]
            for k in range(1, 4):
                p1f[k] = [prev_exp]
            return avd

        def emit_B5(c, avd, dec_per_j=0):
            first = True
            for j in range(16):
                if dec_per_j:
                    emit_decode(dec_per_j)
                ldg = DMA("sync", wpool[:, (j % 2) * 2048:(j % 2 + 1) * 2048], dec["g"][j],
                          waits=[dec_store[("g", j)], wslot_use[j % 2]])
                ldu = DMA("sync", wpool[:, (2 + j % 2) * 2048:(3 + j % 2) * 2048], dec["u"][j],
                          waits=[dec_store[("u", j)], wslot_use[2 + j % 2]])
                for tg in range(4):
                    qb = tg % 2
                    bg = P1[:, (2 * qb) * 512:(2 * qb + 1) * 512]
                    bu = P1[:, (2 * qb + 1) * 512:(2 * qb + 2) * 512]
                    Wl("tensor", [ldg, ldu] + xnt_all[c])
                    Wl("tensor", p1f[2 * qb] + p1f[2 * qb + 1])
                    gmm = umm = None
                    for kd in range(16):
                        gmm = OP("tensor", lambda t, _p=bg, _w=wpool[:, (j % 2) * 2048 + kd * 128:
                                 (j % 2) * 2048 + (kd + 1) * 128],
                                 _x=xnt[:, kd * 2048 + tg * 512:kd * 2048 + (tg + 1) * 512],
                                 st=(kd == 0), sp=(kd == 15): t.matmul(_p, _w, _x, start=st, stop=sp),
                                 incs="pe" if kd == 15 else None) or gmm
                    for kd in range(16):
                        umm = OP("tensor", lambda t, _p=bu, _w=wpool[:, (2 + j % 2) * 2048 + kd * 128:
                                 (2 + j % 2) * 2048 + (kd + 1) * 128],
                                 _x=xnt[:, kd * 2048 + tg * 512:kd * 2048 + (tg + 1) * 512],
                                 st=(kd == 0), sp=(kd == 15): t.matmul(_p, _w, _x, start=st, stop=sp),
                                 incs="pe" if kd == 15 else None) or umm
                    if tg == 3:
                        wslot_use[j % 2] = gmm
                        wslot_use[2 + j % 2] = umm
                    geld = OP("scalar", lambda s, _g=bg, o=gg[:, qb * 512:(qb + 1) * 512]:
                              s.activation(out=o, in_=_g, func=AF.Gelu_apprx_tanh),
                              incs="act", waits=[gmm, gg_free[qb]])
                    p1f[2 * qb] = [geld]
                    iw = [umm, geld]
                    if first:
                        iw += [avd, ao_ev[0]]
                        first = False
                    intd = OP("vector", lambda v, _u=bu,
                              o=uni[:, j * 2048 + tg * 512:j * 2048 + (tg + 1) * 512],
                              g2=gg[:, qb * 512:(qb + 1) * 512]:
                              v.tensor_tensor(o, g2, _u, A.mult), incs="dve", waits=iw)
                    p1f[2 * qb + 1] = [intd]
                    gg_free[qb] = intd
            b5_last[c] = cnt["pe"]
            b5_intd[0] = cnt["dve"]

        def emit_B6(c):
            W("tensor", "act", ao_ev[0][1])
            W("tensor", "dve", b5_intd[0])
            if c == 0:
                W("tensor", "dve", a1_all)
            wo_toks = []
            for kt in range(2):
                wo_toks.append(DMA("sync", wpool[:, (2 + kt) * 2048:(3 + kt) * 2048],
                                   dec_a[6 + kt], waits=a1_st + [wslot_use[2 + kt]]))
            part_last = None
            for dtl in range(16):
                slot = dtl % 2
                ldd = DMA("sync", wpool[:, slot * 2048:(slot + 1) * 2048],
                          dec["d"][dtl],
                          waits=[dec_store[("d", dtl)], wslot_use[slot]])
                for tg in range(4):
                    gs = (4 * dtl + tg) % 2
                    pp = P2[gs]
                    Wl("tensor", [ldd] + wo_toks)
                    Wl("tensor", [p2f[gs]])
                    for kt in range(2):
                        OP("tensor", lambda t, _p=pp[:],
                           _w=wpool[:, (2 + kt) * 2048 + dtl * 128:(2 + kt) * 2048 + (dtl + 1) * 128],
                           _a=ao[:, kt * 2048 + tg * 512:kt * 2048 + (tg + 1) * 512],
                           st=(kt == 0): t.matmul(_p, _w, _a, start=st, stop=False))
                    pdn = None
                    for kn in range(16):
                        pdn = OP("tensor", lambda t, _p=pp[:],
                                 _w=wpool[:, slot * 2048 + kn * 128:slot * 2048 + (kn + 1) * 128],
                                 _i=uni[:, kn * 2048 + tg * 512:kn * 2048 + (tg + 1) * 512],
                                 sp=(kn == 15): t.matmul(_p, _w, _i, start=False, stop=sp),
                                 incs="pe" if kn == 15 else None) or pdn
                    if tg == 3:
                        wslot_use[slot] = pdn
                        if dtl == 15:
                            wslot_use[2] = pdn
                            wslot_use[3] = pdn
                    ypc = OP("scalar", lambda s, o=yp[:, gs * 512:(gs + 1) * 512], i2=pp[:]:
                             s.copy(out=o, in_=i2), incs="act",
                             waits=[pdn, yp_free[gs]])
                    p2f[gs] = ypc
                    part_last = DMA("scalar",
                                    part[c][dtl * 128:(dtl + 1) * 128, tg * 512:(tg + 1) * 512],
                                    yp[:, gs * 512:(gs + 1) * 512], waits=[ypc])
                    yp_free[gs] = part_last
            Wl("gpsimd", qmark("scalar"))
            rs_done[c] = OP("gpsimd", lambda g, _c=c: g.collective_compute(
                "ReduceScatter", A.add, replica_groups=[list(range(R))],
                ins=[part[_c].opt()], outs=[rs_o[_c].opt()]), incs="cc")

        def emit_F(c):
            firstw = [("cc", rs_done[c][1])]
            stv = None
            for pc in range(8):
                dtl, tg = pc // 4, pc % 4
                slot = pc % 2
                rsb = fnb[:, slot * 1536:slot * 1536 + 512]
                xtb = fnb[:, slot * 1536 + 512:slot * 1536 + 1024]
                yb = fnb[:, slot * 1536 + 1024:slot * 1536 + 1536]
                l1 = DMA("gpsimd", rsb,
                         rs_o[c][dtl * 128:(dtl + 1) * 128, tg * 512:(tg + 1) * 512],
                         waits=firstw + [fn_free[slot]])
                firstw = []
                l2 = DMA("gpsimd", xtb,
                         xT_s[dtl * 128:(dtl + 1) * 128, c * 2048 + tg * 512:c * 2048 + (tg + 1) * 512])
                yd = OP("vector", lambda v, _r=rsb, _x=xtb, _y=yb, bi=bo_sb[:, dtl:dtl + 1]:
                        v.scalar_tensor_tensor(_y, _r, bi, _x, A.add, A.add),
                        incs="dve", waits=[l1, l2])
                stv = DMA("gpsimd",
                          yT_s[dtl * 128:(dtl + 1) * 128, c * 2048 + tg * 512:c * 2048 + (tg + 1) * 512],
                          yb, waits=[yd])
                fn_free[slot] = yd
            return stv

        emit_decode(34)
        emit_xnt_load(0)
        emit_B2(0)
        emit_B3(0)
        avd = emit_B4(0, 0, Z)
        avd = emit_B4(0, 1, avd)
        emit_B5(0, avd, dec_per_j=1)
        emit_decode(len(dec_order))
        emit_B6(0)
        for c in range(1, 4):
            emit_xnt_load(c)
            emit_B2(c)
            emit_B3(c)
            avd = emit_B4(c, 0, avd)
            avd = emit_B4(c, 1, avd)
            emit_B5(c, avd)
            emit_B6(c)
            emit_F(c - 1)
        last_st = emit_F(3)
        Wl("gpsimd", [last_st])
        Wl("gpsimd", qmark("gpsimd"))

        with nc.Block() as block:
            @block.sync
            def _(e):
                for f in ops["sync"]:
                    f(e)

            @block.vector
            def _(e):
                for f in ops["vector"]:
                    f(e)

            @block.scalar
            def _(e):
                for f in ops["scalar"]:
                    f(e)

            @block.tensor
            def _(e):
                for f in ops["tensor"]:
                    f(e)

            @block.gpsimd
            def _(e):
                for f in ops["gpsimd"]:
                    f(e)
    return nc


_CACHE = {}


def kernel(L0_attn_x, L0_attn_wq_codes, L0_attn_wq_scales, L0_attn_bq,
           L0_attn_wk_codes, L0_attn_wk_scales, L0_attn_bk,
           L0_attn_wv_codes, L0_attn_wv_scales, L0_attn_bv,
           L0_attn_wo_codes, L0_attn_wo_scales, L0_attn_bo,
           L0_ffn_gate_W_packed, L0_ffn_gate_scales,
           L0_ffn_up_W_packed, L0_ffn_up_scales,
           L0_ffn_down_W_packed, L0_ffn_down_scales):
    if "nc" not in _CACHE:
        _CACHE["nc"] = build_nc()
    nc = _CACHE["nc"]

    ac = np.ascontiguousarray
    xf = np.asarray(L0_attn_x, dtype=np.float32).reshape(T, D)
    xT = ac(xf.T)
    rep = np.repeat(np.arange(64), 32)
    wq_se_f = np.asarray(L0_attn_wq_scales)[rep, :]
    wk_se_f = np.asarray(L0_attn_wk_scales)[rep, :]
    wv_se_f = np.asarray(L0_attn_wv_scales)[rep, :]
    wo_se_f = np.asarray(L0_attn_wo_scales)[rep, :]
    ident = np.eye(128, dtype=ml_dtypes.bfloat16)
    bq = np.asarray(L0_attn_bq)
    bk = np.asarray(L0_attn_bk)
    bv = np.asarray(L0_attn_bv)
    bo = np.asarray(L0_attn_bo)

    in_maps = []
    for c in range(R):
        f0 = FT * c
        n0 = 2048 * c
        m = {
            "x_tok": ac(xf[TKC * c:TKC * (c + 1)]),
            "xT_s": ac(xT[f0:f0 + FT]),
            "wq_c": ac(np.asarray(L0_attn_wq_codes)[:, f0:f0 + FT]),
            "wk_c": ac(np.asarray(L0_attn_wk_codes)[:, f0:f0 + FT]),
            "wv_c": ac(np.asarray(L0_attn_wv_codes)[:, f0:f0 + FT]),
            "wq_se": ac(wq_se_f[:, f0:f0 + FT]),
            "wk_se": ac(wk_se_f[:, f0:f0 + FT]),
            "wv_se": ac(wv_se_f[:, f0:f0 + FT]),
            "wo_c": ac(np.asarray(L0_attn_wo_codes)[f0:f0 + FT, :]),
            "wo_se": ac(wo_se_f[f0:f0 + FT, :]),
            "bqkv": ac(np.stack([bq[f0:f0 + 128], bq[f0 + 128:f0 + 256],
                                 bk[f0:f0 + 128], bk[f0 + 128:f0 + 256],
                                 bv[f0:f0 + 128], bv[f0 + 128:f0 + 256]], axis=1)),
            "bo_s": ac(np.stack([bo[f0:f0 + 128], bo[f0 + 128:f0 + 256]], axis=1)),
            "g_p": ac(np.asarray(L0_ffn_gate_W_packed)[n0:n0 + 2048]),
            "g_s": ac(np.asarray(L0_ffn_gate_scales)[n0:n0 + 2048]),
            "u_p": ac(np.asarray(L0_ffn_up_W_packed)[n0:n0 + 2048]),
            "u_s": ac(np.asarray(L0_ffn_up_scales)[n0:n0 + 2048]),
            "d_p": ac(np.asarray(L0_ffn_down_W_packed)[:, 1024 * c:1024 * (c + 1)]),
            "d_s": ac(np.asarray(L0_ffn_down_scales)[:, 64 * c:64 * (c + 1)]),
            "ident": ident,
        }
        in_maps.append(m)

    res = run_bass_kernel_spmd(nc, in_maps, core_ids=list(range(R)))
    shards = [np.asarray(res.results[c]["yT_s"]) for c in range(R)]
    yT = np.concatenate(shards, axis=0)
    return ac(yT.T).reshape(B_, S, D)


# version 18
# speedup vs baseline: 1.0320x; 1.0090x over previous
import numpy as np
import ml_dtypes
from contextlib import ExitStack
import concourse.bass as bass
import concourse.mybir as mybir
from concourse.bass_utils import run_bass_kernel_spmd

F32 = mybir.dt.float32
BF16 = mybir.dt.bfloat16
I32 = mybir.dt.int32
U16 = mybir.dt.uint16
A = mybir.AluOpType
AF = mybir.ActivationFunctionType
AX = mybir.AxisListType

R = 8
B_, S, D = 4, 2048, 2048
T = B_ * S
TKC = T // R
FT = 256
ISQ = 0.08838834764831845
NQS = 4


def build_nc():
    nc = bass.Bass("TRN2", target_bir_lowering=False)

    def din(name, shape, dt):
        return nc.dram_tensor(name, shape, dt, kind="ExternalInput").ap()

    def dint(name, shape, dt):
        return nc.dram_tensor(name, shape, dt, kind="Internal").ap()

    x_tok = din("x_tok", [TKC, D], F32)
    xT_s = din("xT_s", [FT, T], F32)
    w_qkv_c = [din(f"w{p}_c", [D, FT], F32) for p in "qkv"]
    w_qkv_se = [din(f"w{p}_se", [D, FT], F32) for p in "qkv"]
    wo_c = din("wo_c", [FT, D], F32)
    wo_se = din("wo_se", [FT, D], F32)
    bqkv_h = din("bqkv", [128, 6], F32)
    bo_h = din("bo_s", [128, 2], F32)
    ffn_p = {w: din(f"{w}_p", [2048, 1024], I32) for w in ("g", "u", "d")}
    ffn_s = {w: din(f"{w}_s", [2048, 64], F32) for w in ("g", "u", "d")}
    ident_h = din("ident", [128, 128], BF16)
    yT_s = nc.dram_tensor("yT_s", [FT, T], F32, kind="ExternalOutput").ap()

    ag_in = dint("ag_in", [D, TKC], BF16)
    ag_out = [dint(f"ag_out{h}", [R * 1024, TKC], BF16) for h in range(2)]
    dec = {w: dint(f"dec_{w}", [16, 128, 2048], BF16) for w in ("g", "u", "d")}
    dec_a = dint("dec_a", [8, 128, 2048], BF16)
    part = [[dint(f"part{c}h{h}", [D // 2, S], F32) for h in range(2)] for c in range(4)]
    rs_o = [[dint(f"rs_o{c}h{h}", [128, S], F32) for h in range(2)] for c in range(4)]

    with ExitStack() as ctx:
        def sbuf(name, shape, dt):
            return ctx.enter_context(nc.sbuf_tensor(name, shape, dt))

        def psum(name, shape, dt):
            return ctx.enter_context(nc.psum_tensor(name, shape, dt))

        xnt = sbuf("xnt", [128, 32768], BF16)
        uni = sbuf("uni", [128, 32768], BF16)
        ao = sbuf("ao", [128, 4096], BF16)
        wpool = sbuf("wpool", [128, 8192], BF16)
        apb = sbuf("apb", [128, 4096], BF16)
        pk = sbuf("pk", [128, 2048], I32)
        dtmp = sbuf("dtmp", [128, 4096], I32)
        codes = sbuf("codes", [128, 2048], U16)
        dq = sbuf("dq", [128, 2048], BF16)
        stage = sbuf("stage", [128, 2048], BF16)
        scb = sbuf("scb", [128, 128], F32)
        id_sb = sbuf("id_sb", [128, 128], BF16)
        bqkv_sb = sbuf("bqkv_sb", [128, 6], F32)
        bo_sb = sbuf("bo_sb", [128, 2], F32)
        gg = sbuf("gg", [128, 1024], BF16)
        yp = sbuf("yp", [128, 1024], F32)
        smal = sbuf("smal", [128, 16], F32)
        xn_t = stage

        P1 = psum("P1", [128, 2048], F32)
        P2 = [psum("P2a", [128, 512], F32), psum("P2b", [128, 512], F32)]
        P3 = [psum("P3a", [128, 1024], BF16), psum("P3b", [128, 1024], BF16)]

        qsems = {q: [f"{q}{i}" for i in range(NQS)] for q in ("sync", "scalar", "gpsimd")}
        semnames = ["cc", "pe", "dve", "act"] + sum(qsems.values(), [])
        sem = {n: ctx.enter_context(nc.semaphore(n)) for n in semnames}
        cnt = {n: 0 for n in semnames}
        qrot = {q: 0 for q in qsems}
        ops = {"sync": [], "vector": [], "scalar": [], "tensor": [], "gpsimd": []}

        def em(eng, fn):
            ops[eng].append(fn)

        def W(eng, s, v):
            if v > 0:
                em(eng, lambda e, _s=sem[s], _v=v: e.wait_ge(_s, _v))

        def Wl(eng, toks):
            for (s, v) in toks:
                W(eng, s, v)

        def inc(s, n=1):
            cnt[s] += n
            return cnt[s]

        def DMA(q, out_ap, in_ap, waits=()):
            Wl(q, waits)
            sname = qsems[q][qrot[q] % NQS]
            qrot[q] += 1
            c = inc(sname, 16)
            em(q, lambda e, o=out_ap, i=in_ap, sm=sem[sname]:
               e.dma_start(out=o, in_=i).then_inc(sm, 16))
            return (sname, c)

        def qmark(q):
            return [(s, cnt[s]) for s in qsems[q] if cnt[s] > 0]

        def OP(eng, build, incs=None, waits=()):
            Wl(eng, waits)
            if incs:
                c = inc(incs)
                em(eng, lambda e, b=build, sm=sem[incs]: b(e).then_inc(sm, 1))
                return (incs, c)
            em(eng, lambda e, b=build: b(e))
            return None

        u_f32 = uni[:].bitcast(F32)
        wp_f32 = wpool[:].bitcast(F32)
        jnk = codes[:].bitcast(BF16)
        fnb = dtmp[:].bitcast(F32)

        U_QT, U_KT, U_V, U_P, U_PN, U_VT, U_PT = 0, 4096, 8192, 12288, 14336, 16384, 16384
        Z = ("pe", 0)

        p1f = [[Z], [Z], [Z], [Z]]
        p2f = [Z, Z]
        p3f = [Z, Z]

        for o, i in ((id_sb, ident_h), (bqkv_sb, bqkv_h), (bo_sb, bo_h)):
            DMA("sync", o[:], i)
        con_ld = qmark("sync")

        OP("vector", lambda v: v.memset(smal[:, 10:11], 1e-5))
        OP("vector", lambda v: v.memset(smal[:, 11:12], 16128.0), incs="dve")
        ln_done = []
        pe_tr = Z
        for tt in range(8):
            slot = tt % 2
            xsl = wp_f32[:, slot * 2048:(slot + 1) * 2048]
            ldw = [ln_done[tt - 2]] if tt >= 2 else ()
            ldv = DMA("sync", xsl, x_tok[tt * 128:(tt + 1) * 128, :], waits=ldw)
            rs1 = OP("vector", lambda v, x=xsl: v.reduce_sum(smal[:, 0:1], x, axis=AX.X),
                     incs="dve", waits=[ldv])
            nmn = OP("vector", lambda v: v.tensor_single_scalar(smal[:, 2:3], smal[:, 0:1],
                     -1.0 / D, A.mult), incs="dve", waits=[rs1])
            sq = OP("scalar", lambda s, x=xsl: s.activation(out=jnk[:], in_=x, func=AF.Square,
                    accum_out=smal[:, 1:2]), incs="act", waits=[ldv])
            OP("vector", lambda v: v.tensor_single_scalar(smal[:, 3:4], smal[:, 1:2],
                                                          1.0 / D, A.mult), waits=[sq, nmn])
            m2c = OP("vector", lambda v: v.tensor_tensor(smal[:, 4:5], smal[:, 2:3],
                     smal[:, 2:3], A.mult), incs="dve")
            vd = OP("vector", lambda v: v.tensor_tensor(smal[:, 12:13], smal[:, 3:4],
                    smal[:, 4:5], A.subtract), incs="dve", waits=[m2c])
            sdc = OP("scalar", lambda s: s.activation(out=smal[:, 13:14], in_=smal[:, 12:13],
                     func=AF.Sqrt, bias=smal[:, 10:11], scale=1.0), incs="act", waits=[vd])
            rcc = OP("vector", lambda v: v.reciprocal(smal[:, 5:6], smal[:, 13:14]),
                     incs="dve", waits=[sdc])
            xw = [rcc] + ([pe_tr] if tt >= 1 else [])
            xnd = OP("vector", lambda v, x=xsl: v.tensor_scalar(xn_t[:], x, smal[:, 2:3],
                     smal[:, 5:6], A.add, A.mult), incs="dve", waits=xw)
            ln_done.append(xnd)
            Wl("tensor", [xnd])
            if tt == 0:
                Wl("tensor", con_ld)
            for half in range(2):
                pd = P3[half]
                Wl("tensor", [p3f[half]])
                for k in range(8):
                    dtl = half * 8 + k
                    pe_tr = OP("tensor", lambda t, p=pd[:, k * 128:(k + 1) * 128],
                               i=xn_t[:, dtl * 128:(dtl + 1) * 128]:
                               t.transpose(p, i, id_sb[:]),
                               incs="pe" if k == 7 else None) or pe_tr
                dst = uni[:].rearrange("p (dt q) -> p dt q", q=1024)[
                    :, half * 8:half * 8 + 8, tt * 128:(tt + 1) * 128]
                ev = OP("scalar", lambda s, o=dst,
                        i=pd[:].rearrange("p (dt q) -> p dt q", q=128):
                        s.copy(out=o, in_=i), incs="act", waits=[pe_tr])
                p3f[half] = ev
        W("gpsimd", "act", cnt["act"])
        for dtl in range(16):
            DMA("gpsimd", ag_in[dtl * 128:(dtl + 1) * 128, :],
                uni[:, dtl * 1024:(dtl + 1) * 1024])
        ag_st = qmark("gpsimd")
        Wl("gpsimd", ag_st)
        ag_done = [0, 0]
        for h in range(2):
            ag_done[h] = OP("gpsimd", lambda g, _h=h: g.collective_compute(
                "AllGather", A.bypass, replica_groups=[list(range(R))],
                ins=[ag_in[1024 * _h:1024 * (_h + 1), :].opt()],
                outs=[ag_out[_h].opt()]), incs="cc")

        st_free2 = [("act", 0), ("act", 0)]
        a1_srcs = [(w_qkv_c[0], w_qkv_se[0]), (w_qkv_c[1], w_qkv_se[1]),
                   (w_qkv_c[2], w_qkv_se[2]), (wo_c, wo_se)]
        a1_ttp = [None] * 4
        sidx = 0
        for pi in range(4):
            ch, sh = a1_srcs[pi]
            base = (pi % 2) * 8192
            lw = list(ag_st) if pi < 2 else [a1_ttp[pi - 2]]
            toks = []
            if pi < 3:
                cin = ch[:].rearrange("(dt p) f -> p dt f", p=128)
                sin = sh[:].rearrange("(dt p) f -> p dt f", p=128)
                toks.append(DMA("scalar",
                                u_f32[:, base:base + 4096].rearrange("p (dt f) -> p dt f", f=FT),
                                cin, waits=lw))
                toks.append(DMA("scalar",
                                u_f32[:, base + 4096:base + 8192].rearrange("p (dt f) -> p dt f", f=FT),
                                sin))
            else:
                toks.append(DMA("scalar", u_f32[:, base:base + 2048], ch[0:128, :], waits=lw))
                toks.append(DMA("scalar", u_f32[:, base + 2048:base + 4096], ch[128:256, :]))
                toks.append(DMA("scalar", u_f32[:, base + 4096:base + 6144], sh[0:128, :]))
                toks.append(DMA("scalar", u_f32[:, base + 6144:base + 8192], sh[128:256, :]))
            Wl("vector", toks)
            a1_tt = None
            for f in range(2):
                for hg in range(2):
                    ss = sidx % 2
                    sidx += 1
                    Wl("vector", [st_free2[ss]])
                    for k in range(8):
                        dtl = hg * 8 + k
                        if pi < 3:
                            a_ = u_f32[:, base + dtl * 256 + f * 128:base + dtl * 256 + f * 128 + 128]
                            b_ = u_f32[:, base + 4096 + dtl * 256 + f * 128:base + 4096 + dtl * 256 + f * 128 + 128]
                        else:
                            a_ = u_f32[:, base + f * 2048 + dtl * 128:base + f * 2048 + (dtl + 1) * 128]
                            b_ = u_f32[:, base + 4096 + f * 2048 + dtl * 128:base + 4096 + f * 2048 + (dtl + 1) * 128]
                        o = stage[:, ss * 1024 + k * 128:ss * 1024 + (k + 1) * 128]
                        a1_tt = OP("vector", lambda v, _o=o, _a=a_, _b=b_:
                                   v.tensor_tensor(_o, _a, _b, A.mult),
                                   incs="dve" if k == 7 else None) or a1_tt
                    stv = DMA("scalar", dec_a[2 * pi + f][:, hg * 1024:(hg + 1) * 1024],
                              stage[:, ss * 1024:(ss + 1) * 1024], waits=[a1_tt])
                    st_free2[ss] = stv
            a1_ttp[pi] = a1_tt
        a1_all = cnt["dve"]
        a1_st = qmark("scalar")

        Tt = [dtmp[:, i * 1024:(i + 1) * 1024] for i in range(4)]
        pk_free = [("dve", 0), ("dve", 0)]
        sc_free = [("dve", 0), ("dve", 0)]
        st_free = st_free2
        dec_store = {}
        dec_i = [0]
        dec_order = [(w, rt) for rt in range(16) for w in ("g", "u")] + \
                    [("d", rt) for rt in range(16)]
        p0_act = cnt["act"]

        def emit_decode(n):
            for _ in range(n):
                if dec_i[0] >= len(dec_order):
                    return
                i = dec_i[0]
                dec_i[0] += 1
                w, rt = dec_order[i]
                slot = i % 2
                pks = pk[:, slot * 1024:(slot + 1) * 1024]
                scs = scb[:, slot * 64:(slot + 1) * 64]
                pkv = DMA("gpsimd", pks, ffn_p[w][rt * 128:(rt + 1) * 128, :],
                          waits=[pk_free[slot]])
                scv = DMA("gpsimd", scs, ffn_s[w][rt * 128:(rt + 1) * 128, :],
                          waits=[sc_free[slot]])
                w0 = [pkv] + ([("act", p0_act)] if i == 0 else [])
                dcd = None
                for lohi in range(2):
                    if lohi == 0:
                        OP("vector", lambda v, p=pks: v.tensor_single_scalar(
                           Tt[0][:], p, 7, A.bitwise_and), waits=w0)
                        OP("vector", lambda v, p=pks: v.tensor_single_scalar(
                           Tt[1][:], p, 8, A.bitwise_and))
                    else:
                        OP("vector", lambda v, p=pks: v.tensor_scalar(
                           Tt[0][:], p, 4, 7, A.logical_shift_right, A.bitwise_and))
                        pkfree = OP("vector", lambda v, p=pks: v.tensor_scalar(
                                    Tt[1][:], p, 4, 8, A.logical_shift_right, A.bitwise_and),
                                    incs="dve")
                        pk_free[slot] = pkfree
                    bbv = OP("vector", lambda v: v.tensor_scalar(Tt[2][:], Tt[0][:],
                             64, 16128, A.mult, A.add), incs="dve")
                    mmn = OP("vector", lambda v: v.scalar_tensor_tensor(Tt[3][:], Tt[0][:],
                             16128.0, Tt[2][:], A.mult, A.min), incs="dve", waits=[bbv])
                    dst = codes[:, lohi:2048:2]
                    dcd = OP("vector", lambda v, _d=dst: v.scalar_tensor_tensor(
                             _d, Tt[1][:], 4096.0, Tt[3][:], A.mult, A.add),
                             incs="dve", waits=[mmn])
                dqd = OP("vector", lambda v, _s=scs:
                         v.tensor_tensor(dq[:].rearrange("p (j r) -> p j r", r=32),
                                         codes[:].bitcast(BF16).rearrange("p (j r) -> p j r", r=32),
                                         _s.unsqueeze(2).broadcast_to([128, 64, 32]),
                                         A.mult), incs="dve", waits=[scv, dcd])
                sc_free[slot] = dqd
                Wl("tensor", [dqd])
                for half in range(2):
                    pd = P3[half]
                    Wl("tensor", [p3f[half]])
                    ptr = None
                    for k in range(8):
                        kt2 = half * 8 + k
                        ptr = OP("tensor", lambda t, p=pd[:, k * 128:(k + 1) * 128],
                                 i2=dq[:, kt2 * 128:(kt2 + 1) * 128]:
                                 t.transpose(p, i2, id_sb[:]),
                                 incs="pe" if k == 7 else None) or ptr
                    ss2 = (2 * i + half) % 2
                    ev = OP("scalar", lambda s, o=stage[:, ss2 * 1024:(ss2 + 1) * 1024],
                            i2=pd[:]: s.copy(out=o, in_=i2), incs="act",
                            waits=[ptr, st_free[ss2]])
                    p3f[half] = ev
                    stv = DMA("scalar", dec[w][rt][:, half * 1024:(half + 1) * 1024],
                              stage[:, ss2 * 1024:(ss2 + 1) * 1024], waits=[ev])
                    st_free[ss2] = stv
                dec_store[(w, rt)] = stv

        xnt_h0 = [None] * 4
        xnt_all = [None] * 4
        apb_use = [Z, Z]
        wslot_use = [Z, Z, Z, Z]
        yp_free = [("act", 0), ("act", 0)]
        b5_last = [0, 0, 0, 0]
        rs_done = [0, 0, 0, 0]
        fn_free = [("dve", 0), ("dve", 0)]
        gg_free = [("dve", 0), ("dve", 0)]
        qk_ev = [Z]
        ao_ev = [Z]
        b5_intd = [0]

        def emit_xnt_load(c):
            w0 = [("pe", b5_last[c - 1])] if c > 0 else []
            for half in range(2):
                waits = w0 + ([("cc", ag_done[half][1])] if c == 0 else [])
                w0 = []
                for dtl in range(8 * half, 8 * half + 8):
                    for h2 in range(2):
                        DMA("sync",
                            xnt[:, dtl * 2048 + h2 * 1024:dtl * 2048 + (h2 + 1) * 1024],
                            ag_out[half][(2 * c + h2) * 1024 + (dtl - 8 * half) * 128:
                                         (2 * c + h2) * 1024 + (dtl - 8 * half + 1) * 128, :],
                            waits=waits)
                        waits = ()
                if half == 0:
                    xnt_h0[c] = qmark("sync")
            xnt_all[c] = qmark("sync")

        def emit_B2(c):
            last_ev = Z
            for ft in range(6):
                slot = ft % 2
                apb_ld = DMA("sync", apb[:, slot * 2048:(slot + 1) * 2048],
                             dec_a[ft], waits=a1_st + [apb_use[slot]])
                for tg in range(4):
                    g = (4 * ft + tg) % 2
                    pp = P2[g]
                    Wl("tensor", [apb_ld] + xnt_h0[c])
                    Wl("tensor", [p2f[g]])
                    cp = None
                    for dtl in range(16):
                        if dtl == 8:
                            Wl("tensor", xnt_all[c])
                        cp = OP("tensor", lambda t, _p=pp[:], _w=apb[:, slot * 2048 + dtl * 128:
                                slot * 2048 + (dtl + 1) * 128],
                                _x=xnt[:, dtl * 2048 + tg * 512:dtl * 2048 + (tg + 1) * 512],
                                st=(dtl == 0), sp=(dtl == 15): t.matmul(_p, _w, _x, start=st, stop=sp),
                                incs="pe" if dtl == 15 else None) or cp
                    if tg == 3:
                        apb_use[slot] = cp
                    if ft < 2:
                        dst = uni[:, U_QT + ft * 2048 + tg * 512:U_QT + ft * 2048 + (tg + 1) * 512]
                    elif ft < 4:
                        dst = uni[:, U_KT + (ft - 2) * 2048 + tg * 512:U_KT + (ft - 2) * 2048 + (tg + 1) * 512]
                    else:
                        dst = uni[:, U_VT + (ft - 4) * 2048 + tg * 512:U_VT + (ft - 4) * 2048 + (tg + 1) * 512]
                    ev = OP("scalar", lambda s, o=dst, i2=pp[:], bi=bqkv_sb[:, ft:ft + 1]:
                            s.activation(out=o, in_=i2, func=AF.Identity, bias=bi),
                            incs="act", waits=[cp])
                    p2f[g] = ev
                    last_ev = ev
            qk_ev[0] = last_ev

        def emit_B3(c):
            Wl("tensor", [qk_ev[0]])
            for h in range(2):
                for half in range(2):
                    pd = P3[half]
                    Wl("tensor", [p3f[half]])
                    ptr = None
                    for k in range(8):
                        tk = half * 8 + k
                        ptr = OP("tensor", lambda t, p=pd[:, k * 128:(k + 1) * 128],
                                 i2=uni[:, U_VT + h * 2048 + tk * 128:U_VT + h * 2048 + (tk + 1) * 128]:
                                 t.transpose(p, i2, id_sb[:]),
                                 incs="pe" if k == 7 else None) or ptr
                    dstv = uni[:, U_V + half * 2048:U_V + (half + 1) * 2048].rearrange(
                        "p (a r) -> p a r", r=256)[:, :, h * 128:(h + 1) * 128]
                    ev = OP("scalar", lambda s, o=dstv,
                            i2=pd[:].rearrange("p (a r) -> p a r", r=128):
                            s.copy(out=o, in_=i2), incs="act", waits=[ptr])
                    p3f[half] = ev

        def emit_B4(c, h, avd):
            prev_exp = prev_nm = Z
            ptev = Z
            for q in range(16):
                if q == 0:
                    Wl("tensor", [qk_ev[0]])
                    for k in range(4):
                        Wl("tensor", p1f[k])
                else:
                    Wl("tensor", [prev_exp, prev_nm])
                sc0 = sc3 = None
                for kt in range(4):
                    r_ = OP("tensor", lambda t, _p=P1[:, kt * 512:(kt + 1) * 512],
                            _q=uni[:, U_QT + h * 2048 + q * 128:U_QT + h * 2048 + (q + 1) * 128],
                            _k=uni[:, U_KT + h * 2048 + kt * 512:U_KT + h * 2048 + (kt + 1) * 512]:
                            t.matmul(_p, _q, _k, start=True, stop=True),
                            incs="pe" if kt in (0, 3) else None)
                    if kt == 0:
                        sc0 = r_
                    elif kt == 3:
                        sc3 = r_
                rmx = OP("vector", lambda v: v.reduce_max(smal[:, 6:7], P1[:, 0:512], axis=AX.X,
                         negate=True), incs="dve", waits=[sc0, prev_exp])
                nm = OP("vector", lambda v: v.tensor_single_scalar(smal[:, 7:8], smal[:, 6:7],
                        ISQ, A.mult), incs="dve", waits=[rmx])
                expd = OP("scalar", lambda s: s.activation(out=uni[:, U_P:U_P + 2048],
                          in_=P1[:], func=AF.Exp, scale=ISQ, bias=smal[:, 7:8],
                          accum_out=smal[:, 8:9]), incs="act", waits=[sc3, nm])
                rcp = OP("vector", lambda v: v.reciprocal(smal[:, 9:10], smal[:, 8:9]),
                         incs="dve", waits=[expd])
                pn = OP("vector", lambda v: v.tensor_single_scalar(uni[:, U_PN:U_PN + 2048],
                        uni[:, U_P:U_P + 2048], smal[:, 9:10], A.mult), incs="dve",
                        waits=[rcp])
                prev_exp, prev_nm = expd, nm
                ql = q % 4
                Wl("tensor", [pn])
                for half in range(2):
                    pd = P3[half]
                    Wl("tensor", [p3f[half]])
                    ptr = None
                    for k in range(8):
                        kt = half * 8 + k
                        ptr = OP("tensor", lambda t, p=pd[:, k * 128:(k + 1) * 128],
                                 i2=uni[:, U_PN + kt * 128:U_PN + (kt + 1) * 128]:
                                 t.transpose(p, i2, id_sb[:]),
                                 incs="pe" if k == 7 else None) or ptr
                    dstv = uni[:, U_PT + half * 8 * 512:U_PT + (half * 8 + 8) * 512].rearrange(
                        "p (a r) -> p a r", r=512)[:, :, ql * 128:(ql + 1) * 128]
                    ev = OP("scalar", lambda s, o=dstv,
                            i2=pd[:].rearrange("p (a r) -> p a r", r=128):
                            s.copy(out=o, in_=i2), incs="act",
                            waits=[ptr] + ([avd] if half == 0 else []))
                    p3f[half] = ev
                    ptev = ev
                if ql == 3:
                    g4 = q // 4
                    gs = g4 % 2
                    Wl("tensor", [ptev, p2f[gs]])
                    for kt in range(16):
                        r_ = OP("tensor", lambda t, _p=P2[gs][:],
                                _v=uni[:, U_V + kt * 256 + h * 128:U_V + kt * 256 + (h + 1) * 128],
                                _pt=uni[:, U_PT + kt * 512:U_PT + (kt + 1) * 512],
                                st=(kt == 0), sp=(kt == 15): t.matmul(_p, _v, _pt, start=st, stop=sp),
                                incs="pe" if kt == 15 else None)
                        avd = r_ or avd
                    ev = OP("scalar", lambda s, o=ao[:, h * 2048 + g4 * 512:h * 2048 + (g4 + 1) * 512],
                            i2=P2[gs][:]: s.copy(out=o, in_=i2), incs="act", waits=[avd])
                    p2f[gs] = ev
                    ao_ev[0] = ev
            p1f[0] = [prev_exp, prev_nm]
            for k in range(1, 4):
                p1f[k] = [prev_exp]
            return avd

        def emit_B5(c, avd, dec_per_j=0):
            first = True
            for j in range(16):
                if dec_per_j:
                    emit_decode(dec_per_j)
                ldg = DMA("sync", wpool[:, (j % 2) * 2048:(j % 2 + 1) * 2048], dec["g"][j],
                          waits=[dec_store[("g", j)], wslot_use[j % 2]])
                ldu = DMA("sync", wpool[:, (2 + j % 2) * 2048:(3 + j % 2) * 2048], dec["u"][j],
                          waits=[dec_store[("u", j)], wslot_use[2 + j % 2]])
                for tg in range(4):
                    qb = tg % 2
                    bg = P1[:, (2 * qb) * 512:(2 * qb + 1) * 512]
                    bu = P1[:, (2 * qb + 1) * 512:(2 * qb + 2) * 512]
                    Wl("tensor", [ldg, ldu] + xnt_all[c])
                    Wl("tensor", p1f[2 * qb] + p1f[2 * qb + 1])
                    gmm = umm = None
                    for kd in range(16):
                        gmm = OP("tensor", lambda t, _p=bg, _w=wpool[:, (j % 2) * 2048 + kd * 128:
                                 (j % 2) * 2048 + (kd + 1) * 128],
                                 _x=xnt[:, kd * 2048 + tg * 512:kd * 2048 + (tg + 1) * 512],
                                 st=(kd == 0), sp=(kd == 15): t.matmul(_p, _w, _x, start=st, stop=sp),
                                 incs="pe" if kd == 15 else None) or gmm
                    for kd in range(16):
                        umm = OP("tensor", lambda t, _p=bu, _w=wpool[:, (2 + j % 2) * 2048 + kd * 128:
                                 (2 + j % 2) * 2048 + (kd + 1) * 128],
                                 _x=xnt[:, kd * 2048 + tg * 512:kd * 2048 + (tg + 1) * 512],
                                 st=(kd == 0), sp=(kd == 15): t.matmul(_p, _w, _x, start=st, stop=sp),
                                 incs="pe" if kd == 15 else None) or umm
                    if tg == 3:
                        wslot_use[j % 2] = gmm
                        wslot_use[2 + j % 2] = umm
                    geld = OP("scalar", lambda s, _g=bg, o=gg[:, qb * 512:(qb + 1) * 512]:
                              s.activation(out=o, in_=_g, func=AF.Gelu_apprx_tanh),
                              incs="act", waits=[gmm, gg_free[qb]])
                    p1f[2 * qb] = [geld]
                    iw = [umm, geld]
                    if first:
                        iw += [avd, ao_ev[0]]
                        first = False
                    intd = OP("vector", lambda v, _u=bu,
                              o=uni[:, j * 2048 + tg * 512:j * 2048 + (tg + 1) * 512],
                              g2=gg[:, qb * 512:(qb + 1) * 512]:
                              v.tensor_tensor(o, g2, _u, A.mult), incs="dve", waits=iw)
                    p1f[2 * qb + 1] = [intd]
                    gg_free[qb] = intd
            b5_last[c] = cnt["pe"]
            b5_intd[0] = cnt["dve"]

        def emit_B6(c):
            W("tensor", "act", ao_ev[0][1])
            W("tensor", "dve", b5_intd[0])
            if c == 0:
                W("tensor", "dve", a1_all)
            wo_toks = []
            for kt in range(2):
                wo_toks.append(DMA("sync", wpool[:, (2 + kt) * 2048:(3 + kt) * 2048],
                                   dec_a[6 + kt], waits=a1_st + [wslot_use[2 + kt]]))
            part_last = None
            rs_half = [None, None]
            for idx, dtl in enumerate(list(range(0, 16, 2)) + list(range(1, 16, 2))):
                slot = idx % 2
                ldd = DMA("sync", wpool[:, slot * 2048:(slot + 1) * 2048],
                          dec["d"][dtl],
                          waits=[dec_store[("d", dtl)], wslot_use[slot]])
                for tg in range(4):
                    gs = (4 * dtl + tg) % 2
                    pp = P2[gs]
                    Wl("tensor", [ldd] + wo_toks)
                    Wl("tensor", [p2f[gs]])
                    for kt in range(2):
                        OP("tensor", lambda t, _p=pp[:],
                           _w=wpool[:, (2 + kt) * 2048 + dtl * 128:(2 + kt) * 2048 + (dtl + 1) * 128],
                           _a=ao[:, kt * 2048 + tg * 512:kt * 2048 + (tg + 1) * 512],
                           st=(kt == 0): t.matmul(_p, _w, _a, start=st, stop=False))
                    pdn = None
                    for kn in range(16):
                        pdn = OP("tensor", lambda t, _p=pp[:],
                                 _w=wpool[:, slot * 2048 + kn * 128:slot * 2048 + (kn + 1) * 128],
                                 _i=uni[:, kn * 2048 + tg * 512:kn * 2048 + (tg + 1) * 512],
                                 sp=(kn == 15): t.matmul(_p, _w, _i, start=False, stop=sp),
                                 incs="pe" if kn == 15 else None) or pdn
                    if tg == 3:
                        wslot_use[slot] = pdn
                        if dtl == 15:
                            wslot_use[2] = pdn
                            wslot_use[3] = pdn
                    ypc = OP("scalar", lambda s, o=yp[:, gs * 512:(gs + 1) * 512], i2=pp[:]:
                             s.copy(out=o, in_=i2), incs="act",
                             waits=[pdn, yp_free[gs]])
                    p2f[gs] = ypc
                    part_last = DMA("scalar",
                                    part[c][dtl % 2][(dtl // 2) * 128:(dtl // 2 + 1) * 128,
                                                     tg * 512:(tg + 1) * 512],
                                    yp[:, gs * 512:(gs + 1) * 512], waits=[ypc])
                    yp_free[gs] = part_last
                if idx in (7, 15):
                    h = idx // 8
                    Wl("gpsimd", qmark("scalar"))
                    rs_half[h] = OP("gpsimd", lambda g, _c=c, _h=h: g.collective_compute(
                        "ReduceScatter", A.add, replica_groups=[list(range(R))],
                        ins=[part[_c][_h].opt()], outs=[rs_o[_c][_h].opt()]), incs="cc")
            rs_done[c] = rs_half

        def emit_F(c):
            seen = set()
            stv = None
            for pc in range(8):
                dtl, tg = pc // 4, pc % 4
                slot = pc % 2
                rsb = fnb[:, slot * 1536:slot * 1536 + 512]
                xtb = fnb[:, slot * 1536 + 512:slot * 1536 + 1024]
                yb = fnb[:, slot * 1536 + 1024:slot * 1536 + 1536]
                firstw = [("cc", rs_done[c][dtl][1])] if dtl not in seen else []
                seen.add(dtl)
                l1 = DMA("gpsimd", rsb,
                         rs_o[c][dtl][0:128, tg * 512:(tg + 1) * 512],
                         waits=firstw + [fn_free[slot]])
                l2 = DMA("gpsimd", xtb,
                         xT_s[dtl * 128:(dtl + 1) * 128, c * 2048 + tg * 512:c * 2048 + (tg + 1) * 512])
                yd = OP("vector", lambda v, _r=rsb, _x=xtb, _y=yb, bi=bo_sb[:, dtl:dtl + 1]:
                        v.scalar_tensor_tensor(_y, _r, bi, _x, A.add, A.add),
                        incs="dve", waits=[l1, l2])
                stv = DMA("gpsimd",
                          yT_s[dtl * 128:(dtl + 1) * 128, c * 2048 + tg * 512:c * 2048 + (tg + 1) * 512],
                          yb, waits=[yd])
                fn_free[slot] = yd
            return stv

        emit_decode(48)
        emit_xnt_load(0)
        emit_B2(0)
        emit_B3(0)
        avd = emit_B4(0, 0, Z)
        avd = emit_B4(0, 1, avd)
        emit_B5(0, avd)
        emit_B6(0)
        for c in range(1, 4):
            emit_xnt_load(c)
            emit_B2(c)
            emit_B3(c)
            avd = emit_B4(c, 0, avd)
            avd = emit_B4(c, 1, avd)
            emit_B5(c, avd)
            emit_B6(c)
            emit_F(c - 1)
        last_st = emit_F(3)
        Wl("gpsimd", [last_st])
        Wl("gpsimd", qmark("gpsimd"))

        with nc.Block() as block:
            @block.sync
            def _(e):
                for f in ops["sync"]:
                    f(e)

            @block.vector
            def _(e):
                for f in ops["vector"]:
                    f(e)

            @block.scalar
            def _(e):
                for f in ops["scalar"]:
                    f(e)

            @block.tensor
            def _(e):
                for f in ops["tensor"]:
                    f(e)

            @block.gpsimd
            def _(e):
                for f in ops["gpsimd"]:
                    f(e)
    return nc


_CACHE = {}


def kernel(L0_attn_x, L0_attn_wq_codes, L0_attn_wq_scales, L0_attn_bq,
           L0_attn_wk_codes, L0_attn_wk_scales, L0_attn_bk,
           L0_attn_wv_codes, L0_attn_wv_scales, L0_attn_bv,
           L0_attn_wo_codes, L0_attn_wo_scales, L0_attn_bo,
           L0_ffn_gate_W_packed, L0_ffn_gate_scales,
           L0_ffn_up_W_packed, L0_ffn_up_scales,
           L0_ffn_down_W_packed, L0_ffn_down_scales):
    if "nc" not in _CACHE:
        _CACHE["nc"] = build_nc()
    nc = _CACHE["nc"]

    ac = np.ascontiguousarray
    xf = np.asarray(L0_attn_x, dtype=np.float32).reshape(T, D)
    xT = ac(xf.T)
    rep = np.repeat(np.arange(64), 32)
    wq_se_f = np.asarray(L0_attn_wq_scales)[rep, :]
    wk_se_f = np.asarray(L0_attn_wk_scales)[rep, :]
    wv_se_f = np.asarray(L0_attn_wv_scales)[rep, :]
    wo_se_f = np.asarray(L0_attn_wo_scales)[rep, :]
    ident = np.eye(128, dtype=ml_dtypes.bfloat16)
    bq = np.asarray(L0_attn_bq)
    bk = np.asarray(L0_attn_bk)
    bv = np.asarray(L0_attn_bv)
    bo = np.asarray(L0_attn_bo)

    in_maps = []
    for c in range(R):
        f0 = FT * c
        n0 = 2048 * c
        m = {
            "x_tok": ac(xf[TKC * c:TKC * (c + 1)]),
            "xT_s": ac(xT[f0:f0 + FT]),
            "wq_c": ac(np.asarray(L0_attn_wq_codes)[:, f0:f0 + FT]),
            "wk_c": ac(np.asarray(L0_attn_wk_codes)[:, f0:f0 + FT]),
            "wv_c": ac(np.asarray(L0_attn_wv_codes)[:, f0:f0 + FT]),
            "wq_se": ac(wq_se_f[:, f0:f0 + FT]),
            "wk_se": ac(wk_se_f[:, f0:f0 + FT]),
            "wv_se": ac(wv_se_f[:, f0:f0 + FT]),
            "wo_c": ac(np.asarray(L0_attn_wo_codes)[f0:f0 + FT, :]),
            "wo_se": ac(wo_se_f[f0:f0 + FT, :]),
            "bqkv": ac(np.stack([bq[f0:f0 + 128], bq[f0 + 128:f0 + 256],
                                 bk[f0:f0 + 128], bk[f0 + 128:f0 + 256],
                                 bv[f0:f0 + 128], bv[f0 + 128:f0 + 256]], axis=1)),
            "bo_s": ac(np.stack([bo[f0:f0 + 128], bo[f0 + 128:f0 + 256]], axis=1)),
            "g_p": ac(np.asarray(L0_ffn_gate_W_packed)[n0:n0 + 2048]),
            "g_s": ac(np.asarray(L0_ffn_gate_scales)[n0:n0 + 2048]),
            "u_p": ac(np.asarray(L0_ffn_up_W_packed)[n0:n0 + 2048]),
            "u_s": ac(np.asarray(L0_ffn_up_scales)[n0:n0 + 2048]),
            "d_p": ac(np.asarray(L0_ffn_down_W_packed)[:, 1024 * c:1024 * (c + 1)]),
            "d_s": ac(np.asarray(L0_ffn_down_scales)[:, 64 * c:64 * (c + 1)]),
            "ident": ident,
        }
        in_maps.append(m)

    res = run_bass_kernel_spmd(nc, in_maps, core_ids=list(range(R)))
    shards = [np.asarray(res.results[c]["yT_s"]) for c in range(R)]
    yT = np.concatenate(shards, axis=0)
    return ac(yT.T).reshape(B_, S, D)
